# version 05
# baseline (speedup 1.0000x reference)
import contextlib
import os
import numpy as np
import ml_dtypes
import concourse.bass as bass
import concourse.mybir as mybir
from concourse.bass_utils import run_bass_kernel_spmd
from concourse.bass import IndirectOffsetOnAxis

F32 = mybir.dt.float32
BF16 = mybir.dt.bfloat16
I32 = mybir.dt.int32
AF = mybir.ActivationFunctionType
AX = mybir.AxisListType
ALU = mybir.AluOpType

D = 1024
T = 2048
NB = 16
BPC = 2
G = 8
NC = 8
JS = T // NC
JH = JS // 2
EPS = 1e-5


class Stream:
    def __init__(self, nc, eng, sem, inc):
        self.nc, self.eng, self.sem, self.inc = nc, eng, sem, inc
        self.count = 0
        self.hiwater = {}
        self.dirty = False
        self.selfsync = False

    def emit(self, inst):
        inst.then_inc(self.sem, self.inc)
        self.count += self.inc
        if self.inc == 16:
            self.dirty = True
            return None
        if self.selfsync:
            self.eng.wait_ge(self.sem, self.count)
            self.hiwater[self] = self.count
        return (self, self.count)

    def ckpt(self):
        assert self.inc == 16
        if self.dirty:
            self.eng.wait_ge(self.sem, self.count)
            self.dirty = False
        return (self, self.count)

    def wait(self, token):
        if token is None:
            return
        prod, val = token
        if self.hiwater.get(prod, -1) >= val:
            return
        self.eng.wait_ge(prod.sem, val)
        self.hiwater[prod] = val

    def now(self):
        if self.inc == 16:
            return self.ckpt()
        return (self, self.count)


PHASES = int(os.environ.get("BK_PHASES", "4"))


def build():
    nc = bass.Bass("TRN2", target_bir_lowering=False, num_devices=8)
    stack = contextlib.ExitStack()
    _uctr = [0]
    def un(base):
        _uctr[0] += 1
        return f"{base}_{_uctr[0]}" 

    xin = nc.dram_tensor("x", [BPC, T, D], F32, kind="ExternalInput")
    out = nc.dram_tensor("out", [BPC, T, D], F32, kind="ExternalOutput")
    awT = {}
    for P in ("A", "dt", "B", "C"):
        awT[P] = (
            nc.dram_tensor(f"wmuT_{P}", [D, D], F32, kind="ExternalInput"),
            nc.dram_tensor(f"wlvT_{P}", [D, D], F32, kind="ExternalInput"),
            nc.dram_tensor(f"epsT_{P}", [D, D], F32, kind="ExternalInput"),
        )
    bmu_c = nc.dram_tensor("bmu_cat", [4, D], F32, kind="ExternalInput")
    blv_c = nc.dram_tensor("blv_cat", [4, D], F32, kind="ExternalInput")
    epb_c = nc.dram_tensor("epsb_cat", [4, D], F32, kind="ExternalInput")
    ipwT = nc.dram_tensor("ipwT", [D, 3 * D], F32, kind="ExternalInput")
    ipb = nc.dram_tensor("ipb", [3 * D], F32, kind="ExternalInput")
    opwT = nc.dram_tensor("opwT", [D, D], F32, kind="ExternalInput")
    opb = nc.dram_tensor("opb", [D], F32, kind="ExternalInput")
    Wr = nc.dram_tensor("Wr", [D, 8], F32, kind="ExternalInput")
    br = nc.dram_tensor("br", [1, 8], F32, kind="ExternalInput")
    W12d = nc.dram_tensor("W12", [8 * 128, 16384], F32, kind="ExternalInput")
    rowb = nc.dram_tensor("rowb", [128, 1], F32, kind="ExternalInput")
    b1d = nc.dram_tensor("b1", [8, D], F32, kind="ExternalInput")
    b2d = nc.dram_tensor("b2", [8, D], F32, kind="ExternalInput")
    identf = nc.dram_tensor("identf", [128, 128], F32, kind="ExternalInput")
    identb = nc.dram_tensor("identb", [128, 128], BF16, kind="ExternalInput")
    iota8 = nc.dram_tensor("iota8", [1, 8], F32, kind="ExternalInput")
    maskA = nc.dram_tensor("maskA", [8, 128], BF16, kind="ExternalInput")
    maskB = nc.dram_tensor("maskB", [8, 512], BF16, kind="ExternalInput")
    coreid = nc.dram_tensor("coreid", [1, 1], I32, kind="ExternalInput")

    wbf_scr = nc.dram_tensor("wbf_scr", [G, 128, 4 * D], BF16)
    x2_scr = nc.dram_tensor("x2_scr", [BPC, G, 128, T], F32)
    cc1_in = nc.dram_tensor("cc1_in", [BPC, G, 128, T], BF16)
    cc1_out = nc.dram_tensor("cc1_out", [NC * BPC * G * 128, T], BF16, addr_space="Shared")
    l3s_scr = nc.dram_tensor("l3s_scr", [NB, G, 128, JS], BF16)
    cc2_in = nc.dram_tensor("cc2_in", [NB, G, 128, 2, JH], F32)
    cc2_out = nc.dram_tensor("cc2_out", [NC, NB, G, 128, 2, JH], F32, addr_space="Shared")
    xat_scr = nc.dram_tensor("xat_scr", [BPC, NC, G, 128, 2 * JH], F32)

    t = Stream(nc, nc.tensor, stack.enter_context(nc.semaphore("s_t")), 1)
    v = Stream(nc, nc.vector, stack.enter_context(nc.semaphore("s_v")), 1)
    s = Stream(nc, nc.scalar, stack.enter_context(nc.semaphore("s_s")), 1)
    v.selfsync = True
    s.selfsync = True
    q = Stream(nc, nc.sync, stack.enter_context(nc.semaphore("s_q")), 16)
    g = Stream(nc, nc.gpsimd, stack.enter_context(nc.semaphore("s_g")), 16)
    cc = stack.enter_context(nc.semaphore("s_cc"))
    ALLS = [t, v, s, q, g]

    def barrier():
        toks = [st.now() for st in ALLS]
        for st in ALLS:
            for tok in toks:
                if tok[0] is not st:
                    st.wait(tok)

    blk = stack.enter_context(nc.Block())
    sbuf = lambda name, shape, dt=F32: stack.enter_context(nc.sbuf_tensor(name, shape, dt))

    idf = sbuf("idf", [128, 128], F32)
    idb = sbuf("idb", [128, 128], BF16)
    iot = sbuf("iot", [1, 8], F32)
    mA = sbuf("mA", [8, 128], BF16)
    mB = sbuf("mB", [8, 512], BF16)
    ones_col = sbuf("ones_col", [128, 1], BF16)
    ones_r128 = sbuf("ones_r128", [1, 128], F32)
    ones_row = sbuf("ones_row", [1, 512], F32)
    br_sb = sbuf("br_sb", [1, 8], F32)
    colWr = sbuf("colWr", [1, 8], F32)
    biasA = sbuf("biasA", [128, 8], F32)
    biasB = sbuf("biasB", [128, 8], F32)
    biasC = sbuf("biasC", [128, 8], F32)
    ipb_col = sbuf("ipb_col", [128, 24], F32)
    opb_col = sbuf("opb_col", [128, 8], F32)
    stg = sbuf("stg", [128, 2048], F32)
    wr_sb = sbuf("wr_sb", [128, 64], F32)
    wprep_o = sbuf("wprep_o", [128, 1024], BF16)
    rowR = sbuf("rowR", [1, T], F32)
    rowNMR = sbuf("rowNMR", [1, T], F32)
    sc8 = sbuf("sc8", [1, 8], F32)
    sc8b = sbuf("sc8b", [1, 8], F32)
    sc8c = sbuf("sc8c", [1, 8], F32)
    m11 = sbuf("m11", [1, 1], F32)
    idxf = sbuf("idxf", [1, 2], F32)
    idxi = sbuf("idxi", [1, 2], I32)
    cid_sb = sbuf("cid_sb", [1, 1], I32)
    b1c = [sbuf(f"b1c{i}", [128, 8], F32) for i in range(2)]
    b2row = sbuf("b2row", [1, D], F32)
    b2row2 = sbuf("b2row2", [1, D], F32)
    xbf = sbuf("xbf", [128, 512], BF16)
    sqbf = sbuf("sqbf", [128, 512], BF16)
    srow = sbuf("srow", [1, 1024], F32)
    rowb_sb = sbuf("rowb_sb", [128, 1], F32)
    idxcf = sbuf("idxcf", [128, 1], F32)
    idxci = sbuf("idxci", [128, 1], I32)
    eps11 = sbuf("eps11", [1, 1], F32)
    b1all = sbuf("b1all", [8, 1024], F32)
    b2all = sbuf("b2all", [8, 1024], F32)
    eqA = sbuf("eqA", [1, 8], F32)
    eqB = sbuf("eqB", [1, 8], F32)
    oh82 = sbuf("oh82", [8, 3], F32)

    q.emit(q.eng.dma_start(out=idf[:, :], in_=identf[:, :]))
    q.emit(q.eng.dma_start(out=idb[:, :], in_=identb[:, :]))
    q.emit(q.eng.dma_start(out=iot[:, :], in_=iota8[:, :]))
    q.emit(q.eng.dma_start(out=mA[:, :], in_=maskA[:, :]))
    q.emit(q.eng.dma_start(out=mB[:, :], in_=maskB[:, :]))
    q.emit(q.eng.dma_start(out=br_sb[:, :], in_=br[:, :]))
    g.emit(g.eng.dma_start(out=cid_sb[:, :], in_=coreid[:, :]))
    q.emit(q.eng.dma_start(out=b1all[:, :], in_=b1d[:, :]))
    q.emit(q.eng.dma_start(out=b2all[:, :], in_=b2d[:, :]))
    q.emit(q.eng.dma_start(out=rowb_sb[:, :], in_=rowb[:, :]))
    tok_init = q.ckpt()
    v.wait(tok_init)
    v.emit(v.eng.memset(ones_col[:, :], 1.0))
    v.emit(v.eng.memset(ones_r128[:, :], 1.0))
    v.emit(v.eng.memset(ones_row[:, :], 1.0))
    tok_ones = v.emit(v.eng.memset(eps11[:, :], EPS))

    SKIP_PREP = PHASES <= -1
    prep_stack = contextlib.ExitStack()
    rows8 = prep_stack.enter_context(nc.sbuf_tensor("rows8", [8, 128], F32))
    if SKIP_PREP:
        class _Skip(Exception):
            pass
    rows8b = prep_stack.enter_context(nc.sbuf_tensor("rows8b", [8, 128], F32))
    rows8c = prep_stack.enter_context(nc.sbuf_tensor("rows8c", [8, 128], F32))
    rows24 = prep_stack.enter_context(nc.sbuf_tensor("rows24", [24, 128], F32))
    wprep = [prep_stack.enter_context(nc.sbuf_tensor("wprep0", [128, 1024], F32)), None, None]
    if not SKIP_PREP:
      with nc.psum_tensor("pb_bias", [128, 32], F32) as pbias:
        for i, P in enumerate(("A", "dt", "B", "C")):
            q.emit(q.eng.dma_start(out=rows8[:, :], in_=bmu_c[i:i + 1, :].rearrange("a (blk p) -> (a blk) p", blk=8)))
            q.emit(q.eng.dma_start(out=rows8b[:, :], in_=blv_c[i:i + 1, :].rearrange("a (blk p) -> (a blk) p", blk=8)))
            q.emit(q.eng.dma_start(out=rows8c[:, :], in_=epb_c[i:i + 1, :].rearrange("a (blk p) -> (a blk) p", blk=8)))
            tok1 = tok2 = tok3 = q.ckpt()
            s.wait(tok2)
            toka = s.emit(s.eng.activation(rows8b[:, :], rows8b[:, :], AF.Exp, scale=0.5))
            v.wait(toka); v.wait(tok3)
            v.emit(v.eng.tensor_mul(rows8b[:, :], rows8b[:, :], rows8c[:, :]))
            v.wait(tok1)
            tokv = v.emit(v.eng.tensor_add(rows8b[:, :], rows8b[:, :], rows8[:, :]))
            t.wait(tokv)
            tokt = t.emit(t.eng.transpose(pbias[:, 0:8], rows8b[:, :], idf[0:8, 0:8]))
            v.wait(tokt)
            if P == "A":
                tokb = v.emit(v.eng.tensor_copy(biasA[:, :], pbias[:, 0:8]))
            elif P == "dt":
                tokb = v.emit(v.eng.tensor_add(biasA[:, :], biasA[:, :], pbias[:, 0:8]))
                v.wait(tokb)
            elif P == "B":
                tokb = v.emit(v.eng.tensor_copy(biasB[:, :], pbias[:, 0:8]))
            else:
                tokb = v.emit(v.eng.tensor_scalar(biasC[:, :], pbias[:, 0:8], 16.0, None, ALU.mult))
            q.wait(tokb); t.wait(tokb)
        q.emit(q.eng.dma_start(out=rows24[:, :], in_=ipb.ap().rearrange("(blk p) -> blk p", blk=24)))
        t.wait(q.ckpt())
      with nc.psum_tensor("pb_ipb", [128, 24], F32) as pipb:
        tokt = t.emit(t.eng.transpose(pipb[:, 0:24], rows24[:, :], idf[0:24, 0:24]))
        v.wait(tokt)
        tokb = v.emit(v.eng.tensor_copy(ipb_col[:, :], pipb[:, 0:24]))
        t.wait(tokb)
      with nc.psum_tensor("pb_opb", [128, 8], F32) as popb:
        q.emit(q.eng.dma_start(out=rows8[:, :], in_=opb.ap().rearrange("(blk p) -> blk p", blk=8)))
        t.wait(q.ckpt())
        tokt = t.emit(t.eng.transpose(popb[:, 0:8], rows8[:, :], idf[0:8, 0:8]))
        v.wait(tokt)
        tokb = v.emit(v.eng.tensor_copy(opb_col[:, :], popb[:, 0:8]))
        t.wait(tokb)
    if not SKIP_PREP:
      with nc.psum_tensor("pb_wr", [1, 8], F32) as pwr:
        for kb in range(G):
            q.emit(q.eng.dma_start(out=wprep[0][:, kb * 8:(kb + 1) * 8], in_=Wr[kb * 128:(kb + 1) * 128, :]))
        tokl = q.ckpt()
        v.wait(tokl)
        tokm = v.emit(v.eng.memset(stg[:, 0:1], 1.0))
        t.wait(tokl); t.wait(tokm)
        for kb in range(G):
            tt_ = t.emit(nc.tensor.matmul(pwr[:, :], stg[:, 0:1], wprep[0][:, kb * 8:(kb + 1) * 8], start=(kb == 0), stop=(kb == G - 1)))
        v.wait(tt_)
        v.emit(v.eng.tensor_copy(colWr[:, :], pwr[:, :]))

    prev_store = None
    for P_i, P in enumerate(() if SKIP_PREP else ("A", "dt", "B", "C")):
        wm, wl, we = awT[P]
        for kb in range(G):
            q.wait(s.now()); q.wait(v.now())
            q.emit(q.eng.dma_start(out=wprep[0][:, :], in_=wm[kb * 128:(kb + 1) * 128, :]))
            q.emit(q.eng.dma_start(out=stg[:, 0:1024], in_=wl[kb * 128:(kb + 1) * 128, :]))
            q.emit(q.eng.dma_start(out=stg[:, 1024:2048], in_=we[kb * 128:(kb + 1) * 128, :]))
            tok1 = tok2 = tok3 = q.ckpt()
            s.wait(tok2)
            toka = s.emit(s.eng.activation(stg[:, 0:1024], stg[:, 0:1024], AF.Exp, scale=0.5))
            v.wait(toka); v.wait(tok3)
            v.emit(v.eng.tensor_mul(stg[:, 0:1024], stg[:, 0:1024], stg[:, 1024:2048]))
            v.wait(tok1); v.wait(prev_store)
            tokv = v.emit(v.eng.tensor_add(wprep_o[:, :], stg[:, 0:1024], wprep[0][:, :]))
            g.wait(tokv)
            g.emit(g.eng.dma_start(out=wbf_scr[kb, :, P_i * D:(P_i + 1) * D], in_=wprep_o[:, :]))
            prev_store = g.ckpt()
    barrier()
    prep_stack.close()


    def emit_ln(xg, lnb):
        with (
            nc.psum_tensor(un("pl_sx"), [1, 512], F32) as psx,
            nc.psum_tensor(un("pl_sx2"), [1, 512], F32) as psx2,
            nc.psum_tensor(un("pl_bc1"), [128, 512], F32) as pbc1,
            nc.psum_tensor(un("pl_bc2"), [128, 512], F32) as pbc2,
        ):
            tok_apply = None
            for ch in range(4):
                cs = slice(ch * 512, (ch + 1) * 512)
                for gi in range(G):
                    v.wait(t.now())
                    tok_x = v.emit(v.eng.tensor_copy(xbf[:, :], xg[gi][:, cs]))
                    s.wait(t.now())
                    tok_sq = s.emit(s.eng.activation(sqbf[:, :], xg[gi][:, cs], AF.Square))
                    t.wait(tok_x); t.wait(tok_sq); t.wait(tok_ones)
                    t.emit(nc.tensor.matmul(psx[:, :], ones_col[:, :], xbf[:, :], start=(gi == 0), stop=(gi == G - 1)))
                    t.emit(nc.tensor.matmul(psx2[:, :], ones_col[:, :], sqbf[:, :], start=(gi == 0), stop=(gi == G - 1)))
                tok_mm = t.now()
                v.wait(tok_mm)
                vv = lambda f: v.emit(f())
                tok1 = vv(lambda: v.eng.tensor_scalar(rowNMR[0:1, cs], psx[:, :], 1.0 / D, None, ALU.mult))
                tok2 = vv(lambda: v.eng.tensor_scalar(srow[0:1, 0:512], psx2[:, :], 1.0 / D, None, ALU.mult))
                v.wait(tok1); v.wait(tok2)
                tok3 = vv(lambda: v.eng.scalar_tensor_tensor(srow[0:1, 512:1024], rowNMR[0:1, cs], -1.0, rowNMR[0:1, cs], ALU.mult, ALU.mult))
                v.wait(tok3)
                tok4 = vv(lambda: v.eng.tensor_add(srow[0:1, 0:512], srow[0:1, 0:512], srow[0:1, 512:1024]))
                s.wait(tok4)
                s.wait(tok_ones)
                tok5 = s.emit(s.eng.activation(srow[0:1, 512:1024], srow[0:1, 0:512], AF.Ln, bias=eps11[:, :]))
                s.wait(tok5)
                tok6 = s.emit(s.eng.activation(rowR[0:1, cs], srow[0:1, 512:1024], AF.Exp, scale=-0.5))
                v.wait(tok6)
                tok7 = vv(lambda: v.eng.tensor_mul(rowNMR[0:1, cs], rowNMR[0:1, cs], rowR[0:1, cs]))
                v.wait(tok7)
                tok8 = vv(lambda: v.eng.tensor_scalar(rowNMR[0:1, cs], rowNMR[0:1, cs], -1.0, None, ALU.mult))
                t.wait(tok8); t.wait(tok6)
                t.wait(tok_apply)
                tb1 = t.emit(nc.tensor.matmul(pbc1[:, :], ones_r128[:, :], rowR[0:1, cs], start=True, stop=True))
                tb2 = t.emit(nc.tensor.matmul(pbc2[:, :], ones_r128[:, :], rowNMR[0:1, cs], start=True, stop=True))
                for gi in range(G):
                    v.wait(tb1); v.wait(tb2)
                    v.emit(v.eng.tensor_mul(stg[:, 0:512], xg[gi][:, cs], pbc1[:, :]))
                    tok_apply = v.emit(v.eng.tensor_add(lnb[gi][:, cs], stg[:, 0:512], pbc2[:, :]))
            t.wait(tok_apply)

    def emit_attn_a(xg, lnb):
        a_stack = contextlib.ExitStack()
        scE = a_stack.enter_context(nc.sbuf_tensor(un("scE"), [128, T], F32))
        scB = a_stack.enter_context(nc.sbuf_tensor(un("scB"), [128, T], F32))
        scC = a_stack.enter_context(nc.sbuf_tensor(un("scC"), [128, T], F32))
        scH = a_stack.enter_context(nc.sbuf_tensor(un("scH"), [128, T], F32))
        wcol = [a_stack.enter_context(nc.sbuf_tensor(un(f"wcol{i}"), [128, 4096], BF16)) for i in range(2)]

        def stream_wcol(gi, bi, prev_use):
            g.wait(prev_use)
            for P_ in range(4):
                src = wbf_scr.ap()[:, :, :].rearrange("kb p (P m) -> p kb P m", m=D)[:, :, P_, gi * 128:(gi + 1) * 128]
                dst = wcol[bi][:, P_ * 1024:(P_ + 1) * 1024].rearrange("p (kb m) -> p kb m", kb=8)
                g.emit(g.eng.dma_start(out=dst, in_=src))
            return g.ckpt()

        with (
            nc.psum_tensor(un("pa_ad"), [128, 512], F32) as pAD,
            nc.psum_tensor(un("pa_b"), [128, 512], F32) as pB,
            nc.psum_tensor(un("pa_c"), [128, 512], F32) as pC,
        ):
            wtok = [None, None]
            wtok[0] = stream_wcol(0, 0, t.now())
            last_scan = None
            for gi in range(G):
                if gi + 1 < G:
                    wtok[(gi + 1) % 2] = stream_wcol(gi + 1, (gi + 1) % 2, t.now() if gi >= 1 else None)
                wc = wcol[gi % 2][:, :].rearrange("p (P kb m) -> p P kb m", P=4, kb=8)
                t.wait(wtok[gi % 2])
                s.wait(last_scan)
                v.wait(last_scan)
                for ch in range(4):
                    cs = slice(ch * 512, (ch + 1) * 512)
                    for kb in range(G):
                        t.emit(nc.tensor.matmul(pAD[:, :], wc[:, 0, kb, :], lnb[kb][:, cs], start=(kb == 0), stop=False))
                    for kb in range(G):
                        tAD = t.emit(nc.tensor.matmul(pAD[:, :], wc[:, 1, kb, :], lnb[kb][:, cs], start=False, stop=(kb == G - 1)))
                    for kb in range(G):
                        tB = t.emit(nc.tensor.matmul(pB[:, :], wc[:, 2, kb, :], lnb[kb][:, cs], start=(kb == 0), stop=(kb == G - 1)))
                    for kb in range(G):
                        tC = t.emit(nc.tensor.matmul(pC[:, :], wc[:, 3, kb, :], lnb[kb][:, cs], start=(kb == 0), stop=(kb == G - 1)))
                    s.wait(tAD)
                    tok_u = s.emit(s.eng.activation(sqbf[:, :], pAD[:, :], AF.Exp, bias=biasA[:, gi:gi + 1]))
                    s.wait(tok_u)
                    tok_E = s.emit(s.eng.activation(scE[:, cs], sqbf[:, :], AF.Exp, scale=-1.0))
                    v.wait(tB)
                    tok_Bx = v.emit(v.eng.scalar_tensor_tensor(scB[:, cs], pB[:, :], biasB[:, gi:gi + 1], lnb[gi][:, cs], ALU.add, ALU.mult))
                    s.wait(tC)
                    tok_C = s.emit(s.eng.activation(scC[:, cs], pC[:, :], AF.Identity, bias=biasC[:, gi:gi + 1], scale=16.0))
                    t.wait(tok_u); t.wait(tok_Bx); t.wait(tok_C)
                v.wait(tok_E); v.wait(tok_C)
                tok_h = v.emit(v.eng.tensor_tensor_scan(scH[:, :], scE[:, :], scB[:, :], 0.0, ALU.mult, ALU.add))
                v.wait(tok_h)
                tok_y = v.emit(v.eng.tensor_mul(scH[:, :], scH[:, :], scC[:, :]))
                v.wait(tok_y)
                last_scan = v.emit(v.eng.tensor_add(xg[gi][:, :], xg[gi][:, :], scH[:, :]))
            v.wait(last_scan)
        barrier()
        a_stack.close()

    def emit_routing_top2(xg):
        with nc.psum_tensor(un("pr_t1"), [1, 8], F32) as pt1:
            q.wait(v.now()); q.wait(t.now())
            for kb in range(G):
                q.emit(q.eng.dma_start(out=wr_sb[:, kb * 8:(kb + 1) * 8], in_=Wr[kb * 128:(kb + 1) * 128, :]))
            t.wait(q.ckpt())
            for kb in range(G):
                tmm = t.emit(nc.tensor.matmul(pt1[:, :], xg[kb][:, 0:1], wr_sb[:, kb * 8:(kb + 1) * 8], start=(kb == 0), stop=(kb == G - 1)))
            def vstep(f):
                tok = v.emit(f())
                v.wait(tok)
                return tok
            v.wait(tmm)
            vstep(lambda: v.eng.tensor_scalar(sc8[:, :], pt1[:, :], rowR[0:1, 0:1], None, ALU.mult))
            vstep(lambda: v.eng.scalar_tensor_tensor(sc8b[:, :], colWr[:, :], rowNMR[0:1, 0:1], sc8[:, :], ALU.mult, ALU.add))
            vstep(lambda: v.eng.tensor_add(sc8[:, :], sc8b[:, :], br_sb[:, :]))
            vstep(lambda: v.eng.reduce_max(m11[:, :], sc8[:, :], axis=AX.X))
            vstep(lambda: v.eng.tensor_scalar(eqA[:, :], sc8[:, :], m11[:, :], None, ALU.is_ge))
            vstep(lambda: v.eng.scalar_tensor_tensor(sc8c[:, :], eqA[:, :], -1e30, sc8[:, :], ALU.mult, ALU.add))
            vstep(lambda: v.eng.reduce_max(m11[:, :], sc8c[:, :], axis=AX.X))
            vstep(lambda: v.eng.tensor_scalar(eqB[:, :], sc8c[:, :], m11[:, :], None, ALU.is_ge))
            vstep(lambda: v.eng.tensor_mul(sc8b[:, :], eqA[:, :], iot[:, :]))
            vstep(lambda: v.eng.reduce_sum(idxf[0:1, 0:1], sc8b[:, :], axis=AX.X))
            vstep(lambda: v.eng.tensor_mul(sc8b[:, :], eqB[:, :], iot[:, :]))
            vstep(lambda: v.eng.reduce_sum(idxf[0:1, 1:2], sc8b[:, :], axis=AX.X))
            vstep(lambda: v.eng.tensor_add(sc8b[:, :], eqA[:, :], eqB[:, :]))
            return vstep(lambda: v.eng.tensor_copy(idxi[:, :], idxf[:, :]))

    def emit_ffn(xg, lnb):
        f_stack = contextlib.ExitStack()
        hk = [f_stack.enter_context(nc.sbuf_tensor(un(f"hk{i}"), [128, 1024], BF16)) for i in range(G)]
        w12g = f_stack.enter_context(nc.sbuf_tensor(un("w12g"), [128, 16384], BF16))
        wstage2 = f_stack.enter_context(nc.sbuf_tensor(un("wstage2"), [128, 2048], F32))
        tok_idx = emit_routing_top2(xg)
        with nc.psum_tensor(un("pf_oh0"), [8, 1], F32) as poh0, \
             nc.psum_tensor(un("pf_oh1"), [8, 1], F32) as poh1, \
             nc.psum_tensor(un("pf_oh2"), [8, 1], F32) as poh2:
            t.wait(tok_idx)
            t.emit(t.eng.transpose(poh0[:, 0:1], eqA[:, :], idf[0:1, 0:1]))
            t.emit(t.eng.transpose(poh1[:, 0:1], eqB[:, :], idf[0:1, 0:1]))
            tokt = t.emit(t.eng.transpose(poh2[:, 0:1], sc8b[:, :], idf[0:1, 0:1]))
            v.wait(tokt)
            v.emit(v.eng.tensor_copy(oh82[:, 0:1], poh0[:, 0:1]))
            v.emit(v.eng.tensor_copy(oh82[:, 1:2], poh1[:, 0:1]))
            tok_oh = v.emit(v.eng.tensor_copy(oh82[:, 2:3], poh2[:, 0:1]))
        with nc.psum_tensor(un("pf_b1"), [128, 2], F32) as pb1, \
             nc.psum_tensor(un("pf_b2"), [1, 1024], F32) as pb2:
            t.wait(tok_oh)
            for gi in range(G):
                tokm = t.emit(nc.tensor.matmul(pb1[:, :], b1all[:, gi * 128:(gi + 1) * 128], oh82[:, 0:2], start=True, stop=True))
                v.wait(tokm)
                tokc1 = v.emit(v.eng.tensor_copy(b1c[0][:, gi:gi + 1], pb1[:, 0:1]))
                tokc2 = v.emit(v.eng.tensor_copy(b1c[1][:, gi:gi + 1], pb1[:, 1:2]))
                t.wait(tokc2)
            for hb in range(2):
                tokm = t.emit(nc.tensor.matmul(pb2[:, hb * 512:(hb + 1) * 512], oh82[:, 2:3], b2all[:, hb * 512:(hb + 1) * 512], start=True, stop=True))
            v.wait(tokm)
            tok_b2 = v.emit(v.eng.tensor_copy(b2row[:, :], pb2[:, :]))
            t.wait(tok_b2)
        with (
            nc.psum_tensor(un("pf_h"), [128, 512], F32) as pH,
            nc.psum_tensor(un("pf_o"), [128, 512], F32) as pO,
        ):
            for kk in range(2):
                with nc.psum_tensor(un("pf_e"), [128, 1], F32) as pe1:
                    t.wait(tok_idx); t.wait(tok_ones)
                    tokeb = t.emit(nc.tensor.matmul(pe1[:, :], ones_r128[:, :], idxf[0:1, kk:kk + 1], start=True, stop=True))
                    v.wait(tokeb); v.wait(g.now())
                    toki = v.emit(v.eng.scalar_tensor_tensor(idxcf[:, :], pe1[:, :], 128.0, rowb_sb[:, :], ALU.mult, ALU.add))
                    v.wait(toki)
                    toki = v.emit(v.eng.tensor_copy(idxci[:, :], idxcf[:, :]))
                tok_cast = [None, None]
                tw1 = tw2 = None
                for cchunk in range(8):
                    stgbuf = stg if cchunk % 2 == 0 else wstage2
                    g.wait(toki)
                    g.wait(tok_cast[cchunk % 2])
                    if cchunk < 2:
                        g.wait(t.now())
                    g.emit(g.eng.indirect_dma_start(
                        stgbuf[:, :], None,
                        W12d.ap(),
                        IndirectOffsetOnAxis(idxci[:, :], 0),
                        element_offset=cchunk * 2048))
                    v.wait(g.ckpt())
                    tok_cast[cchunk % 2] = tw1 = tw2 = v.emit(v.eng.tensor_copy(w12g[:, cchunk * 2048:(cchunk + 1) * 2048], stgbuf[:, :]))
                w1v = w12g[:, 0:8192].rearrange("p (kb m) -> p kb m", kb=8)
                w2v = w12g[:, 8192:16384].rearrange("p (kb m) -> p kb m", kb=8)
                for th in range(2):
                    s.wait(t.now())
                    for gi in range(G):
                        for ch in range(2):
                            cs = slice(th * 1024 + ch * 512, th * 1024 + (ch + 1) * 512)
                            hs = slice(ch * 512, (ch + 1) * 512)
                            t.wait(tw1)
                            for kb in range(G):
                                tmm = t.emit(nc.tensor.matmul(pH[:, :], w1v[:, kb, gi * 128:(gi + 1) * 128], lnb[kb][:, cs], start=(kb == 0), stop=(kb == G - 1)))
                            s.wait(tmm)
                            tok_h = s.emit(s.eng.activation(hk[gi][:, hs], pH[:, :], AF.Gelu, bias=b1c[kk][:, gi:gi + 1]))
                            t.wait(tok_h)
                    for gi in range(G):
                        for ch in range(2):
                            cs = slice(th * 1024 + ch * 512, th * 1024 + (ch + 1) * 512)
                            hs = slice(ch * 512, (ch + 1) * 512)
                            t.wait(tw2)
                            if kk == 1:
                                t.wait(tok_b2); t.wait(tok_ones)
                                t.emit(nc.tensor.matmul(pO[:, :], b2row[0:1, gi * 128:(gi + 1) * 128], ones_row[0:1, :], start=True, stop=False, skip_group_check=True))
                            for kb in range(G):
                                tmm = t.emit(nc.tensor.matmul(pO[:, :], w2v[:, kb, gi * 128:(gi + 1) * 128], hk[kb][:, hs], start=(kk == 0 and kb == 0), stop=(kb == G - 1), skip_group_check=True))
                            v.wait(tmm)
                            tok_oe = v.emit(v.eng.scalar_tensor_tensor(xg[gi][:, cs], pO[:, :], 0.5, xg[gi][:, cs], ALU.mult, ALU.add))
                            t.wait(tok_oe)
        barrier()
        f_stack.close()

    with contextlib.ExitStack() as ab_stack:
        sb2 = lambda name, shape, dt=F32: ab_stack.enter_context(nc.sbuf_tensor(name, shape, dt))
        xg = [sb2(f"xg{i}", [128, T], F32) for i in range(G)]
        lnb = [sb2(f"lnb{i}", [128, T], BF16) for i in range(G)]

        for b in range(BPC):
            barrier()
            with nc.psum_tensor(un("pin_tr0"), [128, 512], F32) as ptr0, \
                 nc.psum_tensor(un("pin_tr1"), [128, 512], F32) as ptr1:
                banks = [ptr0, ptr1]
                slot_tok = [None] * 2
                for tt2 in range(T // 128):
                    q.wait(t.now())
                    q.emit(q.eng.dma_start(out=stg[:, 0:D], in_=xin.ap()[b, tt2 * 128:(tt2 + 1) * 128, :]))
                    tokld = q.ckpt()
                    for gi in range(G):
                        slot = gi % 2
                        t.wait(tokld); t.wait(slot_tok[slot])
                        tokt = t.emit(t.eng.transpose(banks[slot][:, 0:128], stg[:, gi * 128:(gi + 1) * 128], idf[:, :]))
                        v.wait(tokt)
                        slot_tok[slot] = v.emit(v.eng.tensor_copy(xg[gi][:, tt2 * 128:(tt2 + 1) * 128], banks[slot][:, 0:128]))
                    q.wait(slot_tok[(G - 1) % 2])
            barrier()
            emit_ln(xg, lnb)
            barrier()
            if PHASES >= 1:
                emit_attn_a(xg, lnb)
                barrier()
            if PHASES >= 2:
                emit_ln(xg, lnb)
                barrier()
                emit_ffn(xg, lnb)
                barrier()
            if PHASES >= -1:
                emit_ln(xg, lnb)
                barrier()
            for gi in range(G):
                if PHASES >= -1:
                    q.emit(q.eng.dma_start(out=cc1_in[b, gi, :, :], in_=lnb[gi][:, :]))
                q.emit(q.eng.dma_start(out=x2_scr[b, gi, :, :], in_=xg[gi][:, :]))
            barrier()

    if PHASES >= 3:
        g.eng.collective_compute(
        "AllGather", ALU.bypass, replica_groups=[list(range(NC))],
        ins=[cc1_in.ap().opt()], outs=[cc1_out.ap().opt()],
        ).then_inc(cc, 1)
        nc.gpsimd.wait_ge(cc, 1)
        barrier()

    if PHASES >= 3:
        with contextlib.ExitStack() as c_stack:
            sb3 = lambda name, shape, dt=BF16: c_stack.enter_context(nc.sbuf_tensor(name, shape, dt))
            qkv = [sb3(f"qkv{i}", [128, 2048]) for i in range(24)]
            vT_sb = sb3("vT_sb", [128, 130])
            expS = sb3("expS", [128, 512])
            rec4 = c_stack.enter_context(nc.sbuf_tensor("rec4", [128, 4], F32))
            aout = c_stack.enter_context(nc.sbuf_tensor("aout", [128, 2048], F32))
            wstr = [sb3(f"wstr{i}", [128, 1024]) for i in range(2)]

            v.emit(v.eng.memset(vT_sb[:, 64:65], 1.0))
            tok_vones = v.emit(v.eng.memset(vT_sb[:, 129:130], 1.0))

            nc.gpsimd.wait_ge(g.sem, g.count)
            with nc.gpsimd.register(un("cid")) as regc:
                nc.gpsimd.reg_load(regc, cid_sb[0:1, 0:1])
                cid = nc.gpsimd.snap(regc, min_val=0, max_val=7)
                off_l3 = cid * JS
                for hv in range(4):
                    g.emit(g.eng.dma_start(
                        out=l3s_scr.ap().rearrange("s g p j -> (s g p) j")[hv * 4096:(hv + 1) * 4096, :],
                        in_=cc1_out.ap()[hv * 4096:(hv + 1) * 4096, bass.ds(off_l3, JS)]))
                tok_l3g = g.ckpt()

                for half in range(2):
                    with contextlib.ExitStack() as l3_stack:
                        ln3jm = [l3_stack.enter_context(nc.sbuf_tensor(un(f"l3j{i}"), [128, 2048], BF16)) for i in range(G)]
                        land = l3_stack.enter_context(nc.sbuf_tensor(un("land"), [128, 2048], BF16))
                        tok_r = None
                        q.wait(tok_l3g)
                        for gi in range(G):
                            srcap = l3s_scr.ap().rearrange("s g p j -> g p s j", g=G)[gi, :, :, half * JH:(half + 1) * JH]
                            q.wait(tok_r)
                            q.emit(q.eng.dma_start(
                                out=land[:, :].rearrange("p (s j) -> p s j", s=16),
                                in_=srcap))
                            v.wait(q.ckpt())
                            tok_r = v.emit(v.eng.tensor_copy(
                                bass.AP(ln3jm[gi], 0, [[2048, 128], [16, 128], [1, 16]]),
                                bass.AP(land, 0, [[2048, 128], [1, 128], [128, 16]]),
                            ))
                        tok_l3 = v.now()
                        with nc.psum_tensor(un("pc_q1"), [128, 512], F32) as pQ1, \
                             nc.psum_tensor(un("pc_q2"), [128, 512], F32) as pQ2:
                            def stream_ipw(mo, bi, prev_use):
                                g.wait(prev_use)
                                src = ipwT.ap()[:, mo * 128:(mo + 1) * 128].rearrange("(kb p) m -> p kb m", p=128)
                                g.emit(g.eng.dma_start(out=wstr[bi][:, :].rearrange("p (kb m) -> p kb m", kb=8), in_=src))
                                return g.ckpt()
                            wtok = [None, None]
                            wtok[0] = stream_ipw(0, 0, t.now())
                            for mo in range(24):
                                if mo + 1 < 24:
                                    wtok[(mo + 1) % 2] = stream_ipw(mo + 1, (mo + 1) % 2, t.now())
                                wv = wstr[mo % 2][:, :].rearrange("p (kb m) -> p kb m", kb=8)
                                pp = pQ1 if mo % 2 == 0 else pQ2
                                t.wait(wtok[mo % 2]); t.wait(tok_l3)
                                for ch in range(4):
                                    cs = slice(ch * 512, (ch + 1) * 512)
                                    for kb in range(G):
                                        tmm = t.emit(nc.tensor.matmul(pp[:, :], wv[:, kb, :], ln3jm[kb][:, cs], start=(kb == 0), stop=(kb == G - 1)))
                                    s.wait(tmm)
                                    tok_e = s.emit(s.eng.activation(qkv[mo][:, cs], pp[:, :], AF.Identity, bias=ipb_col[:, mo:mo + 1]))
                                    t.wait(tok_e)
                    ctx_stack = contextlib.ExitStack()
                    ctx = [ctx_stack.enter_context(nc.sbuf_tensor(un(f"ctx{i}"), [128, 2048], BF16)) for i in range(G)]
                    with nc.psum_tensor(un("pc_sc"), [128, 512], F32) as pS, \
                         nc.psum_tensor(un("pc_av"), [128, 260], F32) as pAV, \
                         nc.psum_tensor(un("pc_vt"), [128, 128], BF16) as pVT, \
                         nc.psum_tensor(un("pc_ct"), [64, 128], BF16) as pCT:
                        v.wait(tok_vones)
                        p_ex = p_av = p_n = p_ct_use = p_vt_use = None
                        for hp in range(8):
                            vt = qkv[16 + hp]
                            qt = qkv[hp]
                            kt = qkv[8 + hp]
                            for jb in range(16):
                                csj = slice(jb * 128, (jb + 1) * 128)
                                t.wait(p_vt_use)
                                tokt = t.emit(t.eng.transpose(pVT[:, :], vt[:, csj], idb[:, :]))
                                s.wait(tokt); s.wait(p_av)
                                c1 = s.emit(s.eng.copy(vT_sb[:, 0:64], pVT[:, 0:64]))
                                p_vt_use = c2 = s.emit(s.eng.copy(vT_sb[:, 65:129], pVT[:, 64:128]))
                                t.wait(p_ex)
                                for hh in range(2):
                                    pcol = slice(hh * 128, (hh + 1) * 128)
                                    qsl = qt[hh * 64:(hh + 1) * 64, csj]
                                    ksl = kt[hh * 64:(hh + 1) * 64, csj]
                                    t.emit(nc.tensor.matmul(pS[:, pcol], mA[:, :], mB[:, 0:128], start=(hh == 0), stop=False, skip_group_check=True))
                                    tsc = t.emit(nc.tensor.matmul(pS[:, pcol], ksl, qsl, start=False, stop=(hh == 1), skip_group_check=True))
                                s.wait(tsc); s.wait(p_av)
                                p_ex = tok_ex = s.emit(s.eng.activation(expS[:, 0:256], pS[:, 0:256], AF.Exp, scale=0.125))
                                t.wait(tok_ex); t.wait(c2); t.wait(p_n)
                                for hh in range(2):
                                    avc = slice(hh * 65, hh * 65 + 65)
                                    p_av = tav = t.emit(nc.tensor.matmul(pAV[:, avc], expS[:, hh * 128:(hh + 1) * 128], vT_sb[:, hh * 65:hh * 65 + 65], start=(hh == 0), stop=(hh == 1), skip_group_check=True))
                                v.wait(tav)
                                tokr = v.emit(v.eng.tensor_copy(rec4[:, 0:2], bass.AP(pAV, 64, [[260, 128], [65, 2]])))
                                v.wait(tokr)
                                tokrc = v.emit(v.eng.reciprocal(rec4[:, 2:4], rec4[:, 0:2]))
                                v.wait(tokrc)
                                for hh in range(2):
                                    nsl = slice(256 + hh * 64, 256 + hh * 64 + 64)
                                    v.wait(p_ct_use)
                                    p_n = tokn = v.emit(v.eng.tensor_scalar(expS[:, nsl], pAV[:, hh * 65:hh * 65 + 64], rec4[:, 2 + hh:3 + hh], None, ALU.mult))
                                    t.wait(tokn)
                                    tokct = t.emit(t.eng.transpose(pCT[:, :], expS[:, nsl], idb[:, :]))
                                    s.wait(tokct)
                                    p_ct_use = toke = s.emit(s.eng.copy(ctx[hp][hh * 64:(hh + 1) * 64, csj], pCT[0:64, :]))
                                    t.wait(toke)
                    with nc.psum_tensor(un("pc_op"), [128, 512], F32) as pOP:
                        def stream_opw(mo, bi, prev_use):
                            g.wait(prev_use)
                            src = opwT.ap()[:, mo * 128:(mo + 1) * 128].rearrange("(kb p) m -> p kb m", p=128)
                            g.emit(g.eng.dma_start(out=wstr[bi][:, :].rearrange("p (kb m) -> p kb m", kb=8), in_=src))
                            return g.ckpt()
                        wtok = [None, None]
                        wtok[0] = stream_opw(0, 0, t.now())
                        tok_w = None
                        for mo in range(G):
                            if mo + 1 < G:
                                wtok[(mo + 1) % 2] = stream_opw(mo + 1, (mo + 1) % 2, t.now())
                            wv = wstr[mo % 2][:, :].rearrange("p (kb m) -> p kb m", kb=8)
                            t.wait(wtok[mo % 2])
                            s.wait(tok_w)
                            for ch in range(4):
                                cs = slice(ch * 512, (ch + 1) * 512)
                                for kb in range(G):
                                    tmm = t.emit(nc.tensor.matmul(pOP[:, :], wv[:, kb, :], ctx[kb][:, cs], start=(kb == 0), stop=(kb == G - 1)))
                                s.wait(tmm)
                                dst = bass.AP(aout, ch * 32, [[2048, 128], [1, 32], [128, 16]])
                                tok_e = s.emit(s.eng.activation(dst, pOP[:, :].rearrange("p (j s) -> p j s", j=32), AF.Identity, bias=opb_col[:, mo:mo + 1]))
                                t.wait(tok_e)
                            q.wait(tok_e)
                            q.emit(q.eng.dma_start(
                                out=cc2_in.ap().rearrange("s g p h j -> g p s h j")[mo, :, :, half, :],
                                in_=aout[:, :].rearrange("p (s j) -> p s j", s=16)))
                            tok_w = q.ckpt()
                    barrier()
                    ctx_stack.close()


    if PHASES >= 3:
        g.eng.collective_compute(
            "AllGather", ALU.bypass, replica_groups=[list(range(NC))],
            ins=[cc2_in.ap().opt()], outs=[cc2_out.ap().opt()],
        ).then_inc(cc, 1)
        nc.gpsimd.wait_ge(cc, 2)
    barrier()

    with contextlib.ExitStack() as d_stack:
        sb4 = lambda name, shape, dt=F32: d_stack.enter_context(nc.sbuf_tensor(name, shape, dt))
        xg = [sb4(f"dxg{i}", [128, T], F32) for i in range(G)]
        lnb = [sb4(f"dlnb{i}", [128, T], BF16) for i in range(G)]

        nc.gpsimd.wait_ge(g.sem, g.count)
        with nc.gpsimd.register(un("cid2")) as regc:
            nc.gpsimd.reg_load(regc, cid_sb[0:1, 0:1])
            cid = nc.gpsimd.snap(regc, min_val=0, max_val=7)
            offb = [cid * 2 + 0, cid * 2 + 1]
            for b in range(BPC):
                if PHASES >= 3:
                    g.emit(g.eng.dma_start(
                        out=xat_scr.ap()[b, :, :, :, :].rearrange("r g p hj -> r (g p hj)"),
                        in_=cc2_out.ap().rearrange("r s g p h j -> r s (g p h j)")[:, bass.ds(offb[b], 1), :].rearrange("r s x -> r (s x)")))
                    tokslab = g.ckpt()
                else:
                    tokslab = None
                barrier()
                for gi in range(G):
                    v.wait(t.now())
                    q.emit(q.eng.dma_start(out=xg[gi][:, :], in_=x2_scr[b, gi, :, :]))
                    if PHASES >= 3:
                        q.wait(tokslab)
                        q.emit(q.eng.dma_start(
                            out=stg[:, :],
                            in_=xat_scr.ap()[b, :, :, :, :].rearrange("r g p hj -> g p r hj")[gi, :, :, :]))
                        tok12 = q.ckpt()
                        v.wait(tok12)
                        tokadd = v.emit(v.eng.tensor_add(xg[gi][:, :], xg[gi][:, :], stg[:, :]))
                        q.wait(tokadd)
                    else:
                        v.wait(q.ckpt())
                barrier()
                if PHASES >= 4:
                    emit_ln(xg, lnb)
                    barrier()
                    emit_ffn(xg, lnb)
                    barrier()
                with nc.psum_tensor(un("pd_tr0"), [128, 512], F32) as ptr0, \
                     nc.psum_tensor(un("pd_tr1"), [128, 512], F32) as ptr1:
                    banks = [ptr0, ptr1]
                    slot_tok = [None] * 2
                    tokw = None
                    for tt2 in range(T // 128):
                        for gi in range(G):
                            slot = gi % 2
                            t.wait(slot_tok[slot])
                            tokt = t.emit(t.eng.transpose(banks[slot][:, 0:128], xg[gi][:, tt2 * 128:(tt2 + 1) * 128], idf[:, :]))
                            v.wait(tokt); v.wait(tokw)
                            slot_tok[slot] = v.emit(v.eng.tensor_copy(stg[:, gi * 128:(gi + 1) * 128], banks[slot][:, 0:128]))
                        q.wait(slot_tok[(G - 1) % 2]); q.wait(v.now())
                        q.emit(q.eng.dma_start(out=out.ap()[b, tt2 * 128:(tt2 + 1) * 128, :], in_=stg[:, 0:D]))
                        tokw = q.ckpt()

    barrier()
    q.ckpt(); g.ckpt()
    stack.close()
    return nc


_NC_CACHE = None


def _get_nc():
    global _NC_CACHE
    if _NC_CACHE is None:
        _NC_CACHE = build()
    return _NC_CACHE


_EPS_CACHE = None


_TF_KEYS = {
    "A": ((1012194634, 3152801799), (1705926158, 899080142)),
    "Bp": ((3187376881, 129218101), (2350016172, 1168365246)),
    "C": ((1605747716, 4117073388), (2853785955, 313133857)),
    "dt": ((2463158877, 4047937370), (1914800406, 1741898942)),
}


def _tf_bits(keypair, n):
    rotl = lambda x, r: ((x << np.uint32(r)) | (x >> np.uint32(32 - r))).astype(np.uint32)
    x0 = np.zeros(n, np.uint32)
    x1 = np.arange(n, dtype=np.uint32)
    ks = [np.uint32(keypair[0]), np.uint32(keypair[1])]
    ks.append(np.uint32(ks[0] ^ ks[1] ^ np.uint32(0x1BD11BDA)))
    x0 = (x0 + ks[0]).astype(np.uint32)
    x1 = (x1 + ks[1]).astype(np.uint32)
    rotations = ((13, 15, 26, 6), (17, 29, 16, 24))
    for d in range(5):
        for r in rotations[d % 2]:
            x0 = (x0 + x1).astype(np.uint32)
            x1 = rotl(x1, r)
            x1 = (x1 ^ x0).astype(np.uint32)
        x0 = (x0 + ks[(d + 1) % 3]).astype(np.uint32)
        x1 = (x1 + ks[(d + 2) % 3] + np.uint32(d + 1)).astype(np.uint32)
    return (x0 ^ x1).astype(np.uint32)


def _erfinv_f32(x):
    x = x.astype(np.float32)
    w = (-np.log1p((-x * x).astype(np.float32))).astype(np.float32)
    ws = (np.sqrt(w.astype(np.float32)) - np.float32(3.0)).astype(np.float32)
    wc = (w - np.float32(2.5)).astype(np.float32)
    pa = np.float32(2.81022636e-08)
    for c in (3.43273939e-07, -3.5233877e-06, -4.39150654e-06, 0.00021858087,
              -0.00125372503, -0.00417768164, 0.246640727, 1.50140941):
        pa = (pa * wc + np.float32(c)).astype(np.float32)
    pb = np.float32(-0.000200214257)
    for c in (0.000100950558, 0.00134934322, -0.00367342844, 0.00573950773,
              -0.0076224613, 0.00943887047, 1.00167406, 2.83297682):
        pb = (pb * ws + np.float32(c)).astype(np.float32)
    p = np.where(w < np.float32(5.0), pa, pb)
    return (p * x).astype(np.float32)


def _tf_normal(keypair, shape):
    bits = _tf_bits(keypair, int(np.prod(shape)))
    fb = (bits >> np.uint32(9)) | np.uint32(0x3F800000)
    u01 = fb.view(np.float32) - np.float32(1.0)
    lo = np.nextafter(np.float32(-1), np.float32(0))
    u = np.maximum(lo, (u01 * (np.float32(1.0) - lo) + lo).astype(np.float32))
    return (np.float32(np.sqrt(2.0)) * _erfinv_f32(u)).astype(np.float32).reshape(shape)


def _eps_constants():
    global _EPS_CACHE
    if _EPS_CACHE is not None:
        return _EPS_CACHE
    eps = {}
    for name in ("A", "Bp", "C", "dt"):
        kw, kb = _TF_KEYS[name]
        eps[name] = (_tf_normal(kw, (D, D)), _tf_normal(kb, (D,)))
    _EPS_CACHE = eps
    return eps


def _common_inputs(ii):
    eps = _eps_constants()
    TT_ = lambda a: np.ascontiguousarray(a.T)
    common = {}
    for P, src in (("A", "A"), ("dt", "dt"), ("B", "Bp"), ("C", "C")):
        common[f"wmuT_{P}"] = TT_(ii[f"L0_attn_a_{src}_wmu"])
        common[f"wlvT_{P}"] = TT_(ii[f"L0_attn_a_{src}_wlv"])
        common[f"epsT_{P}"] = np.ascontiguousarray(eps[src][0].T)
    order = ("A", "dt", "Bp", "C")
    common["bmu_cat"] = np.stack([ii[f"L0_attn_a_{s1}_bmu"] for s1 in order])
    common["blv_cat"] = np.stack([ii[f"L0_attn_a_{s1}_blv"] for s1 in order])
    common["epsb_cat"] = np.stack([eps[s1][1] for s1 in order])
    common["ipwT"] = TT_(ii["L0_attn_b_in_proj_weight"])
    common["ipb"] = ii["L0_attn_b_in_proj_bias"]
    common["opwT"] = TT_(ii["L0_attn_b_out_proj_weight"])
    common["opb"] = ii["L0_attn_b_out_proj_bias"]
    common["Wr"] = ii["L0_ffn_Wr"]
    common["br"] = ii["L0_ffn_br"][None, :]
    w12 = np.concatenate([ii["L0_ffn_W1"], ii["L0_ffn_W2"]], axis=1)
    common["W12"] = np.ascontiguousarray(
        w12.reshape(8, 16, 128, 1024).transpose(0, 2, 1, 3).reshape(8 * 128, 16384))
    common["rowb"] = np.arange(128, dtype=np.float32)[:, None]
    common["b1"] = ii["L0_ffn_b1"]
    common["b2"] = ii["L0_ffn_b2"]
    common["identf"] = np.eye(128, dtype=np.float32)
    common["identb"] = np.eye(128).astype(ml_dtypes.bfloat16)
    common["iota8"] = np.arange(8, dtype=np.float32)[None, :]
    A8 = np.zeros((8, 128), np.float32)
    B8 = np.full((8, 128), -240.0, np.float32)
    for r in range(8):
        A8[r, r * 16:(r + 1) * 16] = 1.0
        B8[r, r * 16:(r + 1) * 16] = 0.0
    common["maskA"] = A8.astype(ml_dtypes.bfloat16)
    common["maskB"] = np.tile(B8, (1, 4)).astype(ml_dtypes.bfloat16)
    return common


def kernel(**inputs):
    ii = {k_: np.asarray(v_) for k_, v_ in inputs.items()}
    common = _common_inputs(ii)
    x = ii["L0_attn_a_x"]
    in_maps = []
    for c in range(NC):
        m = dict(common)
        m["x"] = np.ascontiguousarray(x[2 * c:2 * c + 2])
        m["coreid"] = np.array([[c]], np.int32)
        in_maps.append(m)
    nc = _get_nc()
    res = run_bass_kernel_spmd(nc, in_maps, core_ids=list(range(NC)))
    outp = np.empty((NB, T, D), np.float32)
    for c in range(NC):
        outp[2 * c:2 * c + 2] = res.results[c]["out"]
    return outp


if __name__ == "__main__":
    nc = build()
    print("built ok")


# version 07 (final)
# speedup vs baseline: 1.0481x; 1.0481x over previous
import contextlib
import os
import numpy as np
import ml_dtypes
import concourse.bass as bass
import concourse.mybir as mybir
from concourse.bass_utils import run_bass_kernel_spmd
from concourse.bass import IndirectOffsetOnAxis

F32 = mybir.dt.float32
BF16 = mybir.dt.bfloat16
I32 = mybir.dt.int32
AF = mybir.ActivationFunctionType
AX = mybir.AxisListType
ALU = mybir.AluOpType

D = 1024
T = 2048
NB = 16
BPC = 2
G = 8
NC = 8
JS = T // NC
JH = JS // 2
EPS = 1e-5


class Stream:
    def __init__(self, nc, eng, sem, inc):
        self.nc, self.eng, self.sem, self.inc = nc, eng, sem, inc
        self.count = 0
        self.hiwater = {}
        self.dirty = False
        self.selfsync = False

    def emit(self, inst):
        inst.then_inc(self.sem, self.inc)
        self.count += self.inc
        if self.inc == 16:
            self.dirty = True
            return None
        if self.selfsync:
            self.eng.wait_ge(self.sem, self.count)
            self.hiwater[self] = self.count
        return (self, self.count)

    def ckpt(self):
        assert self.inc == 16
        if self.dirty:
            self.eng.wait_ge(self.sem, self.count)
            self.dirty = False
        return (self, self.count)

    def wait(self, token):
        if token is None:
            return
        prod, val = token
        if self.hiwater.get(prod, -1) >= val:
            return
        self.eng.wait_ge(prod.sem, val)
        self.hiwater[prod] = val

    def now(self):
        if self.inc == 16:
            return self.ckpt()
        return (self, self.count)


PHASES = int(os.environ.get("BK_PHASES", "4"))


def build():
    nc = bass.Bass("TRN2", target_bir_lowering=False, num_devices=8)
    stack = contextlib.ExitStack()
    _uctr = [0]
    def un(base):
        _uctr[0] += 1
        return f"{base}_{_uctr[0]}" 

    xin = nc.dram_tensor("x", [BPC, T, D], F32, kind="ExternalInput")
    out = nc.dram_tensor("out", [BPC, T, D], F32, kind="ExternalOutput")
    awT = {}
    for P in ("A", "dt", "B", "C"):
        awT[P] = (
            nc.dram_tensor(f"wmuT_{P}", [D, D], F32, kind="ExternalInput"),
            nc.dram_tensor(f"wlvT_{P}", [D, D], F32, kind="ExternalInput"),
            nc.dram_tensor(f"epsT_{P}", [D, D], F32, kind="ExternalInput"),
        )
    bmu_c = nc.dram_tensor("bmu_cat", [4, D], F32, kind="ExternalInput")
    blv_c = nc.dram_tensor("blv_cat", [4, D], F32, kind="ExternalInput")
    epb_c = nc.dram_tensor("epsb_cat", [4, D], F32, kind="ExternalInput")
    ipwT = nc.dram_tensor("ipwT", [D, 3 * D], F32, kind="ExternalInput")
    ipb = nc.dram_tensor("ipb", [3 * D], F32, kind="ExternalInput")
    opwT = nc.dram_tensor("opwT", [D, D], F32, kind="ExternalInput")
    opb = nc.dram_tensor("opb", [D], F32, kind="ExternalInput")
    Wr = nc.dram_tensor("Wr", [D, 8], F32, kind="ExternalInput")
    br = nc.dram_tensor("br", [1, 8], F32, kind="ExternalInput")
    W12d = nc.dram_tensor("W12", [8 * 128, 16384], F32, kind="ExternalInput")
    rowb = nc.dram_tensor("rowb", [128, 1], F32, kind="ExternalInput")
    b1d = nc.dram_tensor("b1", [8, D], F32, kind="ExternalInput")
    b2d = nc.dram_tensor("b2", [8, D], F32, kind="ExternalInput")
    identf = nc.dram_tensor("identf", [128, 128], F32, kind="ExternalInput")
    identb = nc.dram_tensor("identb", [128, 128], BF16, kind="ExternalInput")
    iota8 = nc.dram_tensor("iota8", [1, 8], F32, kind="ExternalInput")
    maskA = nc.dram_tensor("maskA", [8, 128], BF16, kind="ExternalInput")
    maskB = nc.dram_tensor("maskB", [8, 512], BF16, kind="ExternalInput")
    coreid = nc.dram_tensor("coreid", [1, 1], I32, kind="ExternalInput")

    wbf_scr = nc.dram_tensor("wbf_scr", [G, 128, 4 * D], BF16)
    x2_scr = nc.dram_tensor("x2_scr", [BPC, G, 128, T], F32)
    cc1_in = nc.dram_tensor("cc1_in", [BPC, G, 128, T], BF16)
    cc1_out = nc.dram_tensor("cc1_out", [NC * BPC * G * 128, T], BF16, addr_space="Shared")
    l3s_scr = nc.dram_tensor("l3s_scr", [NB, G, 128, JS], BF16)
    cc2_in = nc.dram_tensor("cc2_in", [NB, G, 128, 2, JH], F32)
    cc2_out = nc.dram_tensor("cc2_out", [NC, NB, G, 128, 2, JH], F32, addr_space="Shared")
    xat_scr = nc.dram_tensor("xat_scr", [BPC, NC, G, 128, 2 * JH], F32)

    t = Stream(nc, nc.tensor, stack.enter_context(nc.semaphore("s_t")), 1)
    v = Stream(nc, nc.vector, stack.enter_context(nc.semaphore("s_v")), 1)
    s = Stream(nc, nc.scalar, stack.enter_context(nc.semaphore("s_s")), 1)
    v.selfsync = True
    s.selfsync = True
    q = Stream(nc, nc.sync, stack.enter_context(nc.semaphore("s_q")), 16)
    g = Stream(nc, nc.gpsimd, stack.enter_context(nc.semaphore("s_g")), 16)
    cc = stack.enter_context(nc.semaphore("s_cc"))
    ALLS = [t, v, s, q, g]

    def barrier():
        toks = [st.now() for st in ALLS]
        for st in ALLS:
            for tok in toks:
                if tok[0] is not st:
                    st.wait(tok)

    blk = stack.enter_context(nc.Block())
    sbuf = lambda name, shape, dt=F32: stack.enter_context(nc.sbuf_tensor(name, shape, dt))

    idf = sbuf("idf", [128, 128], F32)
    idb = sbuf("idb", [128, 128], BF16)
    iot = sbuf("iot", [1, 8], F32)
    mA = sbuf("mA", [8, 128], BF16)
    mB = sbuf("mB", [8, 512], BF16)
    ones_col = sbuf("ones_col", [128, 1], BF16)
    ones_r128 = sbuf("ones_r128", [1, 128], F32)
    ones_row = sbuf("ones_row", [1, 512], F32)
    br_sb = sbuf("br_sb", [1, 8], F32)
    colWr = sbuf("colWr", [1, 8], F32)
    biasA = sbuf("biasA", [128, 8], F32)
    biasB = sbuf("biasB", [128, 8], F32)
    biasC = sbuf("biasC", [128, 8], F32)
    ipb_col = sbuf("ipb_col", [128, 24], F32)
    opb_col = sbuf("opb_col", [128, 8], F32)
    stg = sbuf("stg", [128, 2048], F32)
    wr_sb = sbuf("wr_sb", [128, 64], F32)
    wprep_o = sbuf("wprep_o", [128, 1024], BF16)
    rowR = sbuf("rowR", [1, T], F32)
    rowNMR = sbuf("rowNMR", [1, T], F32)
    sc8 = sbuf("sc8", [1, 8], F32)
    sc8b = sbuf("sc8b", [1, 8], F32)
    sc8c = sbuf("sc8c", [1, 8], F32)
    m11 = sbuf("m11", [1, 1], F32)
    idxf = sbuf("idxf", [1, 2], F32)
    idxi = sbuf("idxi", [1, 2], I32)
    cid_sb = sbuf("cid_sb", [1, 1], I32)
    b1c = [sbuf(f"b1c{i}", [128, 8], F32) for i in range(2)]
    b2row = sbuf("b2row", [1, D], F32)
    b2row2 = sbuf("b2row2", [1, D], F32)
    xbf = sbuf("xbf", [128, 512], BF16)
    sqbf = sbuf("sqbf", [128, 512], BF16)
    srow = sbuf("srow", [1, 1024], F32)
    rowb_sb = sbuf("rowb_sb", [128, 1], F32)
    idxcf = sbuf("idxcf", [128, 1], F32)
    idxci = sbuf("idxci", [128, 1], I32)
    eps11 = sbuf("eps11", [1, 1], F32)
    b1all = sbuf("b1all", [8, 1024], F32)
    b2all = sbuf("b2all", [8, 1024], F32)
    eqA = sbuf("eqA", [1, 8], F32)
    eqB = sbuf("eqB", [1, 8], F32)
    oh82 = sbuf("oh82", [8, 3], F32)

    q.emit(q.eng.dma_start(out=idf[:, :], in_=identf[:, :]))
    q.emit(q.eng.dma_start(out=idb[:, :], in_=identb[:, :]))
    q.emit(q.eng.dma_start(out=iot[:, :], in_=iota8[:, :]))
    q.emit(q.eng.dma_start(out=mA[:, :], in_=maskA[:, :]))
    q.emit(q.eng.dma_start(out=mB[:, :], in_=maskB[:, :]))
    q.emit(q.eng.dma_start(out=br_sb[:, :], in_=br[:, :]))
    g.emit(g.eng.dma_start(out=cid_sb[:, :], in_=coreid[:, :]))
    q.emit(q.eng.dma_start(out=b1all[:, :], in_=b1d[:, :]))
    q.emit(q.eng.dma_start(out=b2all[:, :], in_=b2d[:, :]))
    q.emit(q.eng.dma_start(out=rowb_sb[:, :], in_=rowb[:, :]))
    tok_init = q.ckpt()
    v.wait(tok_init)
    v.emit(v.eng.memset(ones_col[:, :], 1.0))
    v.emit(v.eng.memset(ones_r128[:, :], 1.0))
    v.emit(v.eng.memset(ones_row[:, :], 1.0))
    tok_ones = v.emit(v.eng.memset(eps11[:, :], EPS))

    SKIP_PREP = PHASES <= -1
    prep_stack = contextlib.ExitStack()
    rows8 = prep_stack.enter_context(nc.sbuf_tensor("rows8", [8, 128], F32))
    if SKIP_PREP:
        class _Skip(Exception):
            pass
    rows8b = prep_stack.enter_context(nc.sbuf_tensor("rows8b", [8, 128], F32))
    rows8c = prep_stack.enter_context(nc.sbuf_tensor("rows8c", [8, 128], F32))
    rows24 = prep_stack.enter_context(nc.sbuf_tensor("rows24", [24, 128], F32))
    wprep = [prep_stack.enter_context(nc.sbuf_tensor("wprep0", [128, 1024], F32)), None, None]
    if not SKIP_PREP:
      with nc.psum_tensor("pb_bias", [128, 32], F32) as pbias:
        for i, P in enumerate(("A", "dt", "B", "C")):
            q.emit(q.eng.dma_start(out=rows8[:, :], in_=bmu_c[i:i + 1, :].rearrange("a (blk p) -> (a blk) p", blk=8)))
            q.emit(q.eng.dma_start(out=rows8b[:, :], in_=blv_c[i:i + 1, :].rearrange("a (blk p) -> (a blk) p", blk=8)))
            q.emit(q.eng.dma_start(out=rows8c[:, :], in_=epb_c[i:i + 1, :].rearrange("a (blk p) -> (a blk) p", blk=8)))
            tok1 = tok2 = tok3 = q.ckpt()
            s.wait(tok2)
            toka = s.emit(s.eng.activation(rows8b[:, :], rows8b[:, :], AF.Exp, scale=0.5))
            v.wait(toka); v.wait(tok3)
            v.emit(v.eng.tensor_mul(rows8b[:, :], rows8b[:, :], rows8c[:, :]))
            v.wait(tok1)
            tokv = v.emit(v.eng.tensor_add(rows8b[:, :], rows8b[:, :], rows8[:, :]))
            t.wait(tokv)
            tokt = t.emit(t.eng.transpose(pbias[:, 0:8], rows8b[:, :], idf[0:8, 0:8]))
            v.wait(tokt)
            if P == "A":
                tokb = v.emit(v.eng.tensor_copy(biasA[:, :], pbias[:, 0:8]))
            elif P == "dt":
                tokb = v.emit(v.eng.tensor_add(biasA[:, :], biasA[:, :], pbias[:, 0:8]))
                v.wait(tokb)
            elif P == "B":
                tokb = v.emit(v.eng.tensor_copy(biasB[:, :], pbias[:, 0:8]))
            else:
                tokb = v.emit(v.eng.tensor_scalar(biasC[:, :], pbias[:, 0:8], 16.0, None, ALU.mult))
            q.wait(tokb); t.wait(tokb)
        q.emit(q.eng.dma_start(out=rows24[:, :], in_=ipb.ap().rearrange("(blk p) -> blk p", blk=24)))
        t.wait(q.ckpt())
      with nc.psum_tensor("pb_ipb", [128, 24], F32) as pipb:
        tokt = t.emit(t.eng.transpose(pipb[:, 0:24], rows24[:, :], idf[0:24, 0:24]))
        v.wait(tokt)
        tokb = v.emit(v.eng.tensor_copy(ipb_col[:, :], pipb[:, 0:24]))
        t.wait(tokb)
      with nc.psum_tensor("pb_opb", [128, 8], F32) as popb:
        q.emit(q.eng.dma_start(out=rows8[:, :], in_=opb.ap().rearrange("(blk p) -> blk p", blk=8)))
        t.wait(q.ckpt())
        tokt = t.emit(t.eng.transpose(popb[:, 0:8], rows8[:, :], idf[0:8, 0:8]))
        v.wait(tokt)
        tokb = v.emit(v.eng.tensor_copy(opb_col[:, :], popb[:, 0:8]))
        t.wait(tokb)
    if not SKIP_PREP:
      with nc.psum_tensor("pb_wr", [1, 8], F32) as pwr:
        for kb in range(G):
            q.emit(q.eng.dma_start(out=wprep[0][:, kb * 8:(kb + 1) * 8], in_=Wr[kb * 128:(kb + 1) * 128, :]))
        tokl = q.ckpt()
        v.wait(tokl)
        tokm = v.emit(v.eng.memset(stg[:, 0:1], 1.0))
        t.wait(tokl); t.wait(tokm)
        for kb in range(G):
            tt_ = t.emit(nc.tensor.matmul(pwr[:, :], stg[:, 0:1], wprep[0][:, kb * 8:(kb + 1) * 8], start=(kb == 0), stop=(kb == G - 1)))
        v.wait(tt_)
        v.emit(v.eng.tensor_copy(colWr[:, :], pwr[:, :]))

    prev_store = None
    for P_i, P in enumerate(() if SKIP_PREP else ("A", "dt", "B", "C")):
        wm, wl, we = awT[P]
        for kb in range(G):
            q.wait(s.now()); q.wait(v.now())
            q.emit(q.eng.dma_start(out=wprep[0][:, :], in_=wm[kb * 128:(kb + 1) * 128, :]))
            q.emit(q.eng.dma_start(out=stg[:, 0:1024], in_=wl[kb * 128:(kb + 1) * 128, :]))
            q.emit(q.eng.dma_start(out=stg[:, 1024:2048], in_=we[kb * 128:(kb + 1) * 128, :]))
            tok1 = tok2 = tok3 = q.ckpt()
            s.wait(tok2)
            toka = s.emit(s.eng.activation(stg[:, 0:1024], stg[:, 0:1024], AF.Exp, scale=0.5))
            v.wait(toka); v.wait(tok3)
            v.emit(v.eng.tensor_mul(stg[:, 0:1024], stg[:, 0:1024], stg[:, 1024:2048]))
            v.wait(tok1); v.wait(prev_store)
            tokv = v.emit(v.eng.tensor_add(wprep_o[:, :], stg[:, 0:1024], wprep[0][:, :]))
            g.wait(tokv)
            g.emit(g.eng.dma_start(out=wbf_scr[kb, :, P_i * D:(P_i + 1) * D], in_=wprep_o[:, :]))
            prev_store = g.ckpt()
    barrier()
    prep_stack.close()


    def emit_ln(xg, lnb):
        with (
            nc.psum_tensor(un("pl_sx"), [1, 512], F32) as psx,
            nc.psum_tensor(un("pl_sx2"), [1, 512], F32) as psx2,
            nc.psum_tensor(un("pl_bc1"), [128, 512], F32) as pbc1,
            nc.psum_tensor(un("pl_bc2"), [128, 512], F32) as pbc2,
        ):
            tok_apply = None
            for ch in range(4):
                cs = slice(ch * 512, (ch + 1) * 512)
                for gi in range(G):
                    v.wait(t.now())
                    tok_x = v.emit(v.eng.tensor_copy(xbf[:, :], xg[gi][:, cs]))
                    s.wait(t.now())
                    tok_sq = s.emit(s.eng.activation(sqbf[:, :], xg[gi][:, cs], AF.Square))
                    t.wait(tok_x); t.wait(tok_sq); t.wait(tok_ones)
                    t.emit(nc.tensor.matmul(psx[:, :], ones_col[:, :], xbf[:, :], start=(gi == 0), stop=(gi == G - 1)))
                    t.emit(nc.tensor.matmul(psx2[:, :], ones_col[:, :], sqbf[:, :], start=(gi == 0), stop=(gi == G - 1)))
                tok_mm = t.now()
                v.wait(tok_mm)
                vv = lambda f: v.emit(f())
                tok1 = vv(lambda: v.eng.tensor_scalar(rowNMR[0:1, cs], psx[:, :], 1.0 / D, None, ALU.mult))
                tok2 = vv(lambda: v.eng.tensor_scalar(srow[0:1, 0:512], psx2[:, :], 1.0 / D, None, ALU.mult))
                v.wait(tok1); v.wait(tok2)
                tok3 = vv(lambda: v.eng.scalar_tensor_tensor(srow[0:1, 512:1024], rowNMR[0:1, cs], -1.0, rowNMR[0:1, cs], ALU.mult, ALU.mult))
                v.wait(tok3)
                tok4 = vv(lambda: v.eng.tensor_add(srow[0:1, 0:512], srow[0:1, 0:512], srow[0:1, 512:1024]))
                s.wait(tok4)
                s.wait(tok_ones)
                tok5 = s.emit(s.eng.activation(srow[0:1, 512:1024], srow[0:1, 0:512], AF.Ln, bias=eps11[:, :]))
                s.wait(tok5)
                tok6 = s.emit(s.eng.activation(rowR[0:1, cs], srow[0:1, 512:1024], AF.Exp, scale=-0.5))
                v.wait(tok6)
                tok7 = vv(lambda: v.eng.tensor_mul(rowNMR[0:1, cs], rowNMR[0:1, cs], rowR[0:1, cs]))
                v.wait(tok7)
                tok8 = vv(lambda: v.eng.tensor_scalar(rowNMR[0:1, cs], rowNMR[0:1, cs], -1.0, None, ALU.mult))
                t.wait(tok8); t.wait(tok6)
                t.wait(tok_apply)
                tb1 = t.emit(nc.tensor.matmul(pbc1[:, :], ones_r128[:, :], rowR[0:1, cs], start=True, stop=True))
                tb2 = t.emit(nc.tensor.matmul(pbc2[:, :], ones_r128[:, :], rowNMR[0:1, cs], start=True, stop=True))
                for gi in range(G):
                    v.wait(tb1); v.wait(tb2)
                    v.emit(v.eng.tensor_mul(stg[:, 0:512], xg[gi][:, cs], pbc1[:, :]))
                    tok_apply = v.emit(v.eng.tensor_add(lnb[gi][:, cs], stg[:, 0:512], pbc2[:, :]))
            t.wait(tok_apply)

    def emit_attn_a(xg, lnb):
        a_stack = contextlib.ExitStack()
        scE = a_stack.enter_context(nc.sbuf_tensor(un("scE"), [128, T], F32))
        scB = a_stack.enter_context(nc.sbuf_tensor(un("scB"), [128, T], F32))
        scC = a_stack.enter_context(nc.sbuf_tensor(un("scC"), [128, T], F32))
        scH = a_stack.enter_context(nc.sbuf_tensor(un("scH"), [128, T], F32))
        wcol = [a_stack.enter_context(nc.sbuf_tensor(un(f"wcol{i}"), [128, 4096], BF16)) for i in range(2)]

        def stream_wcol(gi, bi, prev_use):
            g.wait(prev_use)
            for P_ in range(4):
                src = wbf_scr.ap()[:, :, :].rearrange("kb p (P m) -> p kb P m", m=D)[:, :, P_, gi * 128:(gi + 1) * 128]
                dst = wcol[bi][:, P_ * 1024:(P_ + 1) * 1024].rearrange("p (kb m) -> p kb m", kb=8)
                g.emit(g.eng.dma_start(out=dst, in_=src))
            return g.ckpt()

        with (
            nc.psum_tensor(un("pa_ad"), [128, 512], F32) as pAD,
            nc.psum_tensor(un("pa_b"), [128, 512], F32) as pB,
            nc.psum_tensor(un("pa_c"), [128, 512], F32) as pC,
        ):
            wtok = [None, None]
            wtok[0] = stream_wcol(0, 0, t.now())
            last_scan = None
            for gi in range(G):
                if gi + 1 < G:
                    wtok[(gi + 1) % 2] = stream_wcol(gi + 1, (gi + 1) % 2, t.now() if gi >= 1 else None)
                wc = wcol[gi % 2][:, :].rearrange("p (P kb m) -> p P kb m", P=4, kb=8)
                t.wait(wtok[gi % 2])
                s.wait(last_scan)
                v.wait(last_scan)
                for ch in range(4):
                    cs = slice(ch * 512, (ch + 1) * 512)
                    for kb in range(G):
                        t.emit(nc.tensor.matmul(pAD[:, :], wc[:, 0, kb, :], lnb[kb][:, cs], start=(kb == 0), stop=False))
                    for kb in range(G):
                        tAD = t.emit(nc.tensor.matmul(pAD[:, :], wc[:, 1, kb, :], lnb[kb][:, cs], start=False, stop=(kb == G - 1)))
                    for kb in range(G):
                        tB = t.emit(nc.tensor.matmul(pB[:, :], wc[:, 2, kb, :], lnb[kb][:, cs], start=(kb == 0), stop=(kb == G - 1)))
                    for kb in range(G):
                        tC = t.emit(nc.tensor.matmul(pC[:, :], wc[:, 3, kb, :], lnb[kb][:, cs], start=(kb == 0), stop=(kb == G - 1)))
                    s.wait(tAD)
                    tok_u = s.emit(s.eng.activation(sqbf[:, :], pAD[:, :], AF.Exp, bias=biasA[:, gi:gi + 1]))
                    s.wait(tok_u)
                    tok_E = s.emit(s.eng.activation(scE[:, cs], sqbf[:, :], AF.Exp, scale=-1.0))
                    v.wait(tB)
                    tok_Bx = v.emit(v.eng.scalar_tensor_tensor(scB[:, cs], pB[:, :], biasB[:, gi:gi + 1], lnb[gi][:, cs], ALU.add, ALU.mult))
                    s.wait(tC)
                    tok_C = s.emit(s.eng.activation(scC[:, cs], pC[:, :], AF.Identity, bias=biasC[:, gi:gi + 1], scale=16.0))
                    t.wait(tok_u); t.wait(tok_Bx); t.wait(tok_C)
                v.wait(tok_E); v.wait(tok_C)
                tok_h = v.emit(v.eng.tensor_tensor_scan(scH[:, :], scE[:, :], scB[:, :], 0.0, ALU.mult, ALU.add))
                v.wait(tok_h)
                tok_y = v.emit(v.eng.tensor_mul(scH[:, :], scH[:, :], scC[:, :]))
                v.wait(tok_y)
                last_scan = v.emit(v.eng.tensor_add(xg[gi][:, :], xg[gi][:, :], scH[:, :]))
            v.wait(last_scan)
        barrier()
        a_stack.close()

    def emit_routing_top2(xg):
        with nc.psum_tensor(un("pr_t1"), [1, 8], F32) as pt1:
            q.wait(v.now()); q.wait(t.now())
            for kb in range(G):
                q.emit(q.eng.dma_start(out=wr_sb[:, kb * 8:(kb + 1) * 8], in_=Wr[kb * 128:(kb + 1) * 128, :]))
            t.wait(q.ckpt())
            for kb in range(G):
                tmm = t.emit(nc.tensor.matmul(pt1[:, :], xg[kb][:, 0:1], wr_sb[:, kb * 8:(kb + 1) * 8], start=(kb == 0), stop=(kb == G - 1)))
            def vstep(f):
                tok = v.emit(f())
                v.wait(tok)
                return tok
            v.wait(tmm)
            vstep(lambda: v.eng.tensor_scalar(sc8[:, :], pt1[:, :], rowR[0:1, 0:1], None, ALU.mult))
            vstep(lambda: v.eng.scalar_tensor_tensor(sc8b[:, :], colWr[:, :], rowNMR[0:1, 0:1], sc8[:, :], ALU.mult, ALU.add))
            vstep(lambda: v.eng.tensor_add(sc8[:, :], sc8b[:, :], br_sb[:, :]))
            vstep(lambda: v.eng.reduce_max(m11[:, :], sc8[:, :], axis=AX.X))
            vstep(lambda: v.eng.tensor_scalar(eqA[:, :], sc8[:, :], m11[:, :], None, ALU.is_ge))
            vstep(lambda: v.eng.scalar_tensor_tensor(sc8c[:, :], eqA[:, :], -1e30, sc8[:, :], ALU.mult, ALU.add))
            vstep(lambda: v.eng.reduce_max(m11[:, :], sc8c[:, :], axis=AX.X))
            vstep(lambda: v.eng.tensor_scalar(eqB[:, :], sc8c[:, :], m11[:, :], None, ALU.is_ge))
            vstep(lambda: v.eng.tensor_mul(sc8b[:, :], eqA[:, :], iot[:, :]))
            vstep(lambda: v.eng.reduce_sum(idxf[0:1, 0:1], sc8b[:, :], axis=AX.X))
            vstep(lambda: v.eng.tensor_mul(sc8b[:, :], eqB[:, :], iot[:, :]))
            vstep(lambda: v.eng.reduce_sum(idxf[0:1, 1:2], sc8b[:, :], axis=AX.X))
            vstep(lambda: v.eng.tensor_add(sc8b[:, :], eqA[:, :], eqB[:, :]))
            return vstep(lambda: v.eng.tensor_copy(idxi[:, :], idxf[:, :]))

    def emit_ffn(xg, lnb):
        f_stack = contextlib.ExitStack()
        hk = [f_stack.enter_context(nc.sbuf_tensor(un(f"hk{i}"), [128, 1024], BF16)) for i in range(G)]
        w12g = f_stack.enter_context(nc.sbuf_tensor(un("w12g"), [128, 16384], BF16))
        wstage2 = f_stack.enter_context(nc.sbuf_tensor(un("wstage2"), [128, 2048], F32))
        tok_idx = emit_routing_top2(xg)
        with nc.psum_tensor(un("pf_oh0"), [8, 1], F32) as poh0, \
             nc.psum_tensor(un("pf_oh1"), [8, 1], F32) as poh1, \
             nc.psum_tensor(un("pf_oh2"), [8, 1], F32) as poh2:
            t.wait(tok_idx)
            t.emit(t.eng.transpose(poh0[:, 0:1], eqA[:, :], idf[0:1, 0:1]))
            t.emit(t.eng.transpose(poh1[:, 0:1], eqB[:, :], idf[0:1, 0:1]))
            tokt = t.emit(t.eng.transpose(poh2[:, 0:1], sc8b[:, :], idf[0:1, 0:1]))
            v.wait(tokt)
            v.emit(v.eng.tensor_copy(oh82[:, 0:1], poh0[:, 0:1]))
            v.emit(v.eng.tensor_copy(oh82[:, 1:2], poh1[:, 0:1]))
            tok_oh = v.emit(v.eng.tensor_copy(oh82[:, 2:3], poh2[:, 0:1]))
        with nc.psum_tensor(un("pf_b1"), [128, 2], F32) as pb1, \
             nc.psum_tensor(un("pf_b2"), [1, 1024], F32) as pb2:
            t.wait(tok_oh)
            for gi in range(G):
                tokm = t.emit(nc.tensor.matmul(pb1[:, :], b1all[:, gi * 128:(gi + 1) * 128], oh82[:, 0:2], start=True, stop=True))
                v.wait(tokm)
                tokc1 = v.emit(v.eng.tensor_copy(b1c[0][:, gi:gi + 1], pb1[:, 0:1]))
                tokc2 = v.emit(v.eng.tensor_copy(b1c[1][:, gi:gi + 1], pb1[:, 1:2]))
                t.wait(tokc2)
            for hb in range(2):
                tokm = t.emit(nc.tensor.matmul(pb2[:, hb * 512:(hb + 1) * 512], oh82[:, 2:3], b2all[:, hb * 512:(hb + 1) * 512], start=True, stop=True))
            v.wait(tokm)
            tok_b2 = v.emit(v.eng.tensor_copy(b2row[:, :], pb2[:, :]))
            t.wait(tok_b2)
        with (
            nc.psum_tensor(un("pf_h0"), [128, 512], F32) as pH0,
            nc.psum_tensor(un("pf_h1"), [128, 512], F32) as pH1,
            nc.psum_tensor(un("pf_o0"), [128, 512], F32) as pO0,
            nc.psum_tensor(un("pf_o1"), [128, 512], F32) as pO1,
        ):
            pHs = [pH0, pH1]
            pOs = [pO0, pO1]
            tok_h_bank = [None, None]
            tok_o_bank = [None, None]
            for kk in range(2):
                with nc.psum_tensor(un("pf_e"), [128, 1], F32) as pe1:
                    t.wait(tok_idx); t.wait(tok_ones)
                    tokeb = t.emit(nc.tensor.matmul(pe1[:, :], ones_r128[:, :], idxf[0:1, kk:kk + 1], start=True, stop=True))
                    v.wait(tokeb); v.wait(g.now())
                    toki = v.emit(v.eng.scalar_tensor_tensor(idxcf[:, :], pe1[:, :], 128.0, rowb_sb[:, :], ALU.mult, ALU.add))
                    v.wait(toki)
                    toki = v.emit(v.eng.tensor_copy(idxci[:, :], idxcf[:, :]))
                tok_cast = [None, None]
                tw1 = tw2 = None
                for cchunk in range(8):
                    stgbuf = stg if cchunk % 2 == 0 else wstage2
                    g.wait(toki)
                    g.wait(tok_cast[cchunk % 2])
                    if cchunk < 2:
                        g.wait(t.now())
                    g.emit(g.eng.indirect_dma_start(
                        stgbuf[:, :], None,
                        W12d.ap(),
                        IndirectOffsetOnAxis(idxci[:, :], 0),
                        element_offset=cchunk * 2048))
                    v.wait(g.ckpt())
                    tok_cast[cchunk % 2] = tw1 = tw2 = v.emit(v.eng.tensor_copy(w12g[:, cchunk * 2048:(cchunk + 1) * 2048], stgbuf[:, :]))
                w1v = w12g[:, 0:8192].rearrange("p (kb m) -> p kb m", kb=8)
                w2v = w12g[:, 8192:16384].rearrange("p (kb m) -> p kb m", kb=8)
                for th in range(2):
                    s.wait(t.now())
                    for gi in range(G):
                        for ch in range(2):
                            cs = slice(th * 1024 + ch * 512, th * 1024 + (ch + 1) * 512)
                            hs = slice(ch * 512, (ch + 1) * 512)
                            bank = (gi * 2 + ch) % 2
                            pH = pHs[bank]
                            t.wait(tw1)
                            t.wait(tok_h_bank[bank])
                            for kb in range(G):
                                tmm = t.emit(nc.tensor.matmul(pH[:, :], w1v[:, kb, gi * 128:(gi + 1) * 128], lnb[kb][:, cs], start=(kb == 0), stop=(kb == G - 1)))
                            s.wait(tmm)
                            tok_h_bank[bank] = s.emit(s.eng.activation(hk[gi][:, hs], pH[:, :], AF.Gelu, bias=b1c[kk][:, gi:gi + 1]))
                    t.wait(tok_h_bank[0]); t.wait(tok_h_bank[1])
                    for gi in range(G):
                        for ch in range(2):
                            cs = slice(th * 1024 + ch * 512, th * 1024 + (ch + 1) * 512)
                            hs = slice(ch * 512, (ch + 1) * 512)
                            bank = (gi * 2 + ch) % 2
                            pO = pOs[bank]
                            t.wait(tw2)
                            t.wait(tok_o_bank[bank])
                            if kk == 1:
                                t.wait(tok_b2); t.wait(tok_ones)
                                t.emit(nc.tensor.matmul(pO[:, :], b2row[0:1, gi * 128:(gi + 1) * 128], ones_row[0:1, :], start=True, stop=False, skip_group_check=True))
                            for kb in range(G):
                                tmm = t.emit(nc.tensor.matmul(pO[:, :], w2v[:, kb, gi * 128:(gi + 1) * 128], hk[kb][:, hs], start=(kk == 0 and kb == 0), stop=(kb == G - 1), skip_group_check=True))
                            v.wait(tmm)
                            tok_o_bank[bank] = v.emit(v.eng.scalar_tensor_tensor(xg[gi][:, cs], pO[:, :], 0.5, xg[gi][:, cs], ALU.mult, ALU.add))
        barrier()
        f_stack.close()

    with contextlib.ExitStack() as ab_stack:
        sb2 = lambda name, shape, dt=F32: ab_stack.enter_context(nc.sbuf_tensor(name, shape, dt))
        xg = [sb2(f"xg{i}", [128, T], F32) for i in range(G)]
        lnb = [sb2(f"lnb{i}", [128, T], BF16) for i in range(G)]

        for b in range(BPC):
            barrier()
            with nc.psum_tensor(un("pin_tr0"), [128, 512], F32) as ptr0, \
                 nc.psum_tensor(un("pin_tr1"), [128, 512], F32) as ptr1:
                banks = [ptr0, ptr1]
                slot_tok = [None] * 2
                for tt2 in range(T // 128):
                    q.wait(t.now())
                    q.emit(q.eng.dma_start(out=stg[:, 0:D], in_=xin.ap()[b, tt2 * 128:(tt2 + 1) * 128, :]))
                    tokld = q.ckpt()
                    for gi in range(G):
                        slot = gi % 2
                        t.wait(tokld); t.wait(slot_tok[slot])
                        tokt = t.emit(t.eng.transpose(banks[slot][:, 0:128], stg[:, gi * 128:(gi + 1) * 128], idf[:, :]))
                        v.wait(tokt)
                        slot_tok[slot] = v.emit(v.eng.tensor_copy(xg[gi][:, tt2 * 128:(tt2 + 1) * 128], banks[slot][:, 0:128]))
                    q.wait(slot_tok[(G - 1) % 2])
            barrier()
            emit_ln(xg, lnb)
            barrier()
            if PHASES >= 1:
                emit_attn_a(xg, lnb)
                barrier()
            if PHASES >= 2:
                emit_ln(xg, lnb)
                barrier()
                emit_ffn(xg, lnb)
                barrier()
            if PHASES >= -1:
                emit_ln(xg, lnb)
                barrier()
            for gi in range(G):
                if PHASES >= -1:
                    q.emit(q.eng.dma_start(out=cc1_in[b, gi, :, :], in_=lnb[gi][:, :]))
                q.emit(q.eng.dma_start(out=x2_scr[b, gi, :, :], in_=xg[gi][:, :]))
            barrier()

    if PHASES >= 3:
        g.eng.collective_compute(
        "AllGather", ALU.bypass, replica_groups=[list(range(NC))],
        ins=[cc1_in.ap().opt()], outs=[cc1_out.ap().opt()],
        ).then_inc(cc, 1)
        nc.gpsimd.wait_ge(cc, 1)
        barrier()

    if PHASES >= 3:
        with contextlib.ExitStack() as c_stack:
            sb3 = lambda name, shape, dt=BF16: c_stack.enter_context(nc.sbuf_tensor(name, shape, dt))
            qkv = [sb3(f"qkv{i}", [128, 2048]) for i in range(24)]
            vT_sb = sb3("vT_sb", [128, 130])
            expS = sb3("expS", [128, 512])
            rec4 = c_stack.enter_context(nc.sbuf_tensor("rec4", [128, 4], F32))
            aout = c_stack.enter_context(nc.sbuf_tensor("aout", [128, 2048], F32))
            wstr = [sb3(f"wstr{i}", [128, 1024]) for i in range(2)]

            v.emit(v.eng.memset(vT_sb[:, 64:65], 1.0))
            tok_vones = v.emit(v.eng.memset(vT_sb[:, 129:130], 1.0))

            nc.gpsimd.wait_ge(g.sem, g.count)
            with nc.gpsimd.register(un("cid")) as regc:
                nc.gpsimd.reg_load(regc, cid_sb[0:1, 0:1])
                cid = nc.gpsimd.snap(regc, min_val=0, max_val=7)
                off_l3 = cid * JS
                for hv in range(4):
                    g.emit(g.eng.dma_start(
                        out=l3s_scr.ap().rearrange("s g p j -> (s g p) j")[hv * 4096:(hv + 1) * 4096, :],
                        in_=cc1_out.ap()[hv * 4096:(hv + 1) * 4096, bass.ds(off_l3, JS)]))
                tok_l3g = g.ckpt()

                for half in range(2):
                    with contextlib.ExitStack() as l3_stack:
                        ln3jm = [l3_stack.enter_context(nc.sbuf_tensor(un(f"l3j{i}"), [128, 2048], BF16)) for i in range(G)]
                        land = l3_stack.enter_context(nc.sbuf_tensor(un("land"), [128, 2048], BF16))
                        tok_r = None
                        q.wait(tok_l3g)
                        for gi in range(G):
                            srcap = l3s_scr.ap().rearrange("s g p j -> g p s j", g=G)[gi, :, :, half * JH:(half + 1) * JH]
                            q.wait(tok_r)
                            q.emit(q.eng.dma_start(
                                out=land[:, :].rearrange("p (s j) -> p s j", s=16),
                                in_=srcap))
                            v.wait(q.ckpt())
                            tok_r = v.emit(v.eng.tensor_copy(
                                bass.AP(ln3jm[gi], 0, [[2048, 128], [16, 128], [1, 16]]),
                                bass.AP(land, 0, [[2048, 128], [1, 128], [128, 16]]),
                            ))
                        tok_l3 = v.now()
                        with nc.psum_tensor(un("pc_q1"), [128, 512], F32) as pQ1, \
                             nc.psum_tensor(un("pc_q2"), [128, 512], F32) as pQ2:
                            def stream_ipw(mo, bi, prev_use):
                                g.wait(prev_use)
                                src = ipwT.ap()[:, mo * 128:(mo + 1) * 128].rearrange("(kb p) m -> p kb m", p=128)
                                g.emit(g.eng.dma_start(out=wstr[bi][:, :].rearrange("p (kb m) -> p kb m", kb=8), in_=src))
                                return g.ckpt()
                            wtok = [None, None]
                            wtok[0] = stream_ipw(0, 0, t.now())
                            for mo in range(24):
                                if mo + 1 < 24:
                                    wtok[(mo + 1) % 2] = stream_ipw(mo + 1, (mo + 1) % 2, t.now())
                                wv = wstr[mo % 2][:, :].rearrange("p (kb m) -> p kb m", kb=8)
                                pp = pQ1 if mo % 2 == 0 else pQ2
                                t.wait(wtok[mo % 2]); t.wait(tok_l3)
                                for ch in range(4):
                                    cs = slice(ch * 512, (ch + 1) * 512)
                                    for kb in range(G):
                                        tmm = t.emit(nc.tensor.matmul(pp[:, :], wv[:, kb, :], ln3jm[kb][:, cs], start=(kb == 0), stop=(kb == G - 1)))
                                    s.wait(tmm)
                                    tok_e = s.emit(s.eng.activation(qkv[mo][:, cs], pp[:, :], AF.Identity, bias=ipb_col[:, mo:mo + 1]))
                                    t.wait(tok_e)
                    ctx_stack = contextlib.ExitStack()
                    ctx = [ctx_stack.enter_context(nc.sbuf_tensor(un(f"ctx{i}"), [128, 2048], BF16)) for i in range(G)]
                    with nc.psum_tensor(un("pc_sc"), [128, 512], F32) as pS, \
                         nc.psum_tensor(un("pc_av"), [128, 260], F32) as pAV, \
                         nc.psum_tensor(un("pc_vt"), [128, 128], BF16) as pVT, \
                         nc.psum_tensor(un("pc_ct"), [64, 128], BF16) as pCT:
                        v.wait(tok_vones)
                        p_ex = p_av = p_n = p_ct_use = p_vt_use = None
                        for hp in range(8):
                            vt = qkv[16 + hp]
                            qt = qkv[hp]
                            kt = qkv[8 + hp]
                            for jb in range(16):
                                csj = slice(jb * 128, (jb + 1) * 128)
                                t.wait(p_vt_use)
                                tokt = t.emit(t.eng.transpose(pVT[:, :], vt[:, csj], idb[:, :]))
                                s.wait(tokt); s.wait(p_av)
                                c1 = s.emit(s.eng.copy(vT_sb[:, 0:64], pVT[:, 0:64]))
                                p_vt_use = c2 = s.emit(s.eng.copy(vT_sb[:, 65:129], pVT[:, 64:128]))
                                t.wait(p_ex)
                                for hh in range(2):
                                    pcol = slice(hh * 128, (hh + 1) * 128)
                                    qsl = qt[hh * 64:(hh + 1) * 64, csj]
                                    ksl = kt[hh * 64:(hh + 1) * 64, csj]
                                    t.emit(nc.tensor.matmul(pS[:, pcol], mA[:, :], mB[:, 0:128], start=(hh == 0), stop=False, skip_group_check=True))
                                    tsc = t.emit(nc.tensor.matmul(pS[:, pcol], ksl, qsl, start=False, stop=(hh == 1), skip_group_check=True))
                                s.wait(tsc); s.wait(p_av)
                                p_ex = tok_ex = s.emit(s.eng.activation(expS[:, 0:256], pS[:, 0:256], AF.Exp, scale=0.125))
                                t.wait(tok_ex); t.wait(c2); t.wait(p_n)
                                for hh in range(2):
                                    avc = slice(hh * 65, hh * 65 + 65)
                                    p_av = tav = t.emit(nc.tensor.matmul(pAV[:, avc], expS[:, hh * 128:(hh + 1) * 128], vT_sb[:, hh * 65:hh * 65 + 65], start=(hh == 0), stop=(hh == 1), skip_group_check=True))
                                v.wait(tav)
                                tokr = v.emit(v.eng.tensor_copy(rec4[:, 0:2], bass.AP(pAV, 64, [[260, 128], [65, 2]])))
                                v.wait(tokr)
                                tokrc = v.emit(v.eng.reciprocal(rec4[:, 2:4], rec4[:, 0:2]))
                                v.wait(tokrc)
                                for hh in range(2):
                                    nsl = slice(256 + hh * 64, 256 + hh * 64 + 64)
                                    v.wait(p_ct_use)
                                    p_n = tokn = v.emit(v.eng.tensor_scalar(expS[:, nsl], pAV[:, hh * 65:hh * 65 + 64], rec4[:, 2 + hh:3 + hh], None, ALU.mult))
                                    t.wait(tokn)
                                    tokct = t.emit(t.eng.transpose(pCT[:, :], expS[:, nsl], idb[:, :]))
                                    s.wait(tokct)
                                    p_ct_use = toke = s.emit(s.eng.copy(ctx[hp][hh * 64:(hh + 1) * 64, csj], pCT[0:64, :]))
                                    t.wait(toke)
                    with nc.psum_tensor(un("pc_op"), [128, 512], F32) as pOP:
                        def stream_opw(mo, bi, prev_use):
                            g.wait(prev_use)
                            src = opwT.ap()[:, mo * 128:(mo + 1) * 128].rearrange("(kb p) m -> p kb m", p=128)
                            g.emit(g.eng.dma_start(out=wstr[bi][:, :].rearrange("p (kb m) -> p kb m", kb=8), in_=src))
                            return g.ckpt()
                        wtok = [None, None]
                        wtok[0] = stream_opw(0, 0, t.now())
                        tok_w = None
                        for mo in range(G):
                            if mo + 1 < G:
                                wtok[(mo + 1) % 2] = stream_opw(mo + 1, (mo + 1) % 2, t.now())
                            wv = wstr[mo % 2][:, :].rearrange("p (kb m) -> p kb m", kb=8)
                            t.wait(wtok[mo % 2])
                            s.wait(tok_w)
                            for ch in range(4):
                                cs = slice(ch * 512, (ch + 1) * 512)
                                for kb in range(G):
                                    tmm = t.emit(nc.tensor.matmul(pOP[:, :], wv[:, kb, :], ctx[kb][:, cs], start=(kb == 0), stop=(kb == G - 1)))
                                s.wait(tmm)
                                dst = bass.AP(aout, ch * 32, [[2048, 128], [1, 32], [128, 16]])
                                tok_e = s.emit(s.eng.activation(dst, pOP[:, :].rearrange("p (j s) -> p j s", j=32), AF.Identity, bias=opb_col[:, mo:mo + 1]))
                                t.wait(tok_e)
                            q.wait(tok_e)
                            q.emit(q.eng.dma_start(
                                out=cc2_in.ap().rearrange("s g p h j -> g p s h j")[mo, :, :, half, :],
                                in_=aout[:, :].rearrange("p (s j) -> p s j", s=16)))
                            tok_w = q.ckpt()
                    barrier()
                    ctx_stack.close()


    if PHASES >= 3:
        g.eng.collective_compute(
            "AllGather", ALU.bypass, replica_groups=[list(range(NC))],
            ins=[cc2_in.ap().opt()], outs=[cc2_out.ap().opt()],
        ).then_inc(cc, 1)
        nc.gpsimd.wait_ge(cc, 2)
    barrier()

    with contextlib.ExitStack() as d_stack:
        sb4 = lambda name, shape, dt=F32: d_stack.enter_context(nc.sbuf_tensor(name, shape, dt))
        xg = [sb4(f"dxg{i}", [128, T], F32) for i in range(G)]
        lnb = [sb4(f"dlnb{i}", [128, T], BF16) for i in range(G)]

        nc.gpsimd.wait_ge(g.sem, g.count)
        with nc.gpsimd.register(un("cid2")) as regc:
            nc.gpsimd.reg_load(regc, cid_sb[0:1, 0:1])
            cid = nc.gpsimd.snap(regc, min_val=0, max_val=7)
            offb = [cid * 2 + 0, cid * 2 + 1]
            for b in range(BPC):
                if PHASES >= 3:
                    g.emit(g.eng.dma_start(
                        out=xat_scr.ap()[b, :, :, :, :].rearrange("r g p hj -> r (g p hj)"),
                        in_=cc2_out.ap().rearrange("r s g p h j -> r s (g p h j)")[:, bass.ds(offb[b], 1), :].rearrange("r s x -> r (s x)")))
                    tokslab = g.ckpt()
                else:
                    tokslab = None
                barrier()
                for gi in range(G):
                    v.wait(t.now())
                    q.emit(q.eng.dma_start(out=xg[gi][:, :], in_=x2_scr[b, gi, :, :]))
                    if PHASES >= 3:
                        q.wait(tokslab)
                        q.emit(q.eng.dma_start(
                            out=stg[:, :],
                            in_=xat_scr.ap()[b, :, :, :, :].rearrange("r g p hj -> g p r hj")[gi, :, :, :]))
                        tok12 = q.ckpt()
                        v.wait(tok12)
                        tokadd = v.emit(v.eng.tensor_add(xg[gi][:, :], xg[gi][:, :], stg[:, :]))
                        q.wait(tokadd)
                    else:
                        v.wait(q.ckpt())
                barrier()
                if PHASES >= 4:
                    emit_ln(xg, lnb)
                    barrier()
                    emit_ffn(xg, lnb)
                    barrier()
                with nc.psum_tensor(un("pd_tr0"), [128, 512], F32) as ptr0, \
                     nc.psum_tensor(un("pd_tr1"), [128, 512], F32) as ptr1:
                    banks = [ptr0, ptr1]
                    slot_tok = [None] * 2
                    tokw = None
                    for tt2 in range(T // 128):
                        for gi in range(G):
                            slot = gi % 2
                            t.wait(slot_tok[slot])
                            tokt = t.emit(t.eng.transpose(banks[slot][:, 0:128], xg[gi][:, tt2 * 128:(tt2 + 1) * 128], idf[:, :]))
                            v.wait(tokt); v.wait(tokw)
                            slot_tok[slot] = v.emit(v.eng.tensor_copy(stg[:, gi * 128:(gi + 1) * 128], banks[slot][:, 0:128]))
                        q.wait(slot_tok[(G - 1) % 2]); q.wait(v.now())
                        q.emit(q.eng.dma_start(out=out.ap()[b, tt2 * 128:(tt2 + 1) * 128, :], in_=stg[:, 0:D]))
                        tokw = q.ckpt()

    barrier()
    q.ckpt(); g.ckpt()
    stack.close()
    return nc


_NC_CACHE = None


def _get_nc():
    global _NC_CACHE
    if _NC_CACHE is None:
        _NC_CACHE = build()
    return _NC_CACHE


_EPS_CACHE = None


_TF_KEYS = {
    "A": ((1012194634, 3152801799), (1705926158, 899080142)),
    "Bp": ((3187376881, 129218101), (2350016172, 1168365246)),
    "C": ((1605747716, 4117073388), (2853785955, 313133857)),
    "dt": ((2463158877, 4047937370), (1914800406, 1741898942)),
}


def _tf_bits(keypair, n):
    rotl = lambda x, r: ((x << np.uint32(r)) | (x >> np.uint32(32 - r))).astype(np.uint32)
    x0 = np.zeros(n, np.uint32)
    x1 = np.arange(n, dtype=np.uint32)
    ks = [np.uint32(keypair[0]), np.uint32(keypair[1])]
    ks.append(np.uint32(ks[0] ^ ks[1] ^ np.uint32(0x1BD11BDA)))
    x0 = (x0 + ks[0]).astype(np.uint32)
    x1 = (x1 + ks[1]).astype(np.uint32)
    rotations = ((13, 15, 26, 6), (17, 29, 16, 24))
    for d in range(5):
        for r in rotations[d % 2]:
            x0 = (x0 + x1).astype(np.uint32)
            x1 = rotl(x1, r)
            x1 = (x1 ^ x0).astype(np.uint32)
        x0 = (x0 + ks[(d + 1) % 3]).astype(np.uint32)
        x1 = (x1 + ks[(d + 2) % 3] + np.uint32(d + 1)).astype(np.uint32)
    return (x0 ^ x1).astype(np.uint32)


def _erfinv_f32(x):
    x = x.astype(np.float32)
    w = (-np.log1p((-x * x).astype(np.float32))).astype(np.float32)
    ws = (np.sqrt(w.astype(np.float32)) - np.float32(3.0)).astype(np.float32)
    wc = (w - np.float32(2.5)).astype(np.float32)
    pa = np.float32(2.81022636e-08)
    for c in (3.43273939e-07, -3.5233877e-06, -4.39150654e-06, 0.00021858087,
              -0.00125372503, -0.00417768164, 0.246640727, 1.50140941):
        pa = (pa * wc + np.float32(c)).astype(np.float32)
    pb = np.float32(-0.000200214257)
    for c in (0.000100950558, 0.00134934322, -0.00367342844, 0.00573950773,
              -0.0076224613, 0.00943887047, 1.00167406, 2.83297682):
        pb = (pb * ws + np.float32(c)).astype(np.float32)
    p = np.where(w < np.float32(5.0), pa, pb)
    return (p * x).astype(np.float32)


def _tf_normal(keypair, shape):
    bits = _tf_bits(keypair, int(np.prod(shape)))
    fb = (bits >> np.uint32(9)) | np.uint32(0x3F800000)
    u01 = fb.view(np.float32) - np.float32(1.0)
    lo = np.nextafter(np.float32(-1), np.float32(0))
    u = np.maximum(lo, (u01 * (np.float32(1.0) - lo) + lo).astype(np.float32))
    return (np.float32(np.sqrt(2.0)) * _erfinv_f32(u)).astype(np.float32).reshape(shape)


def _eps_constants():
    global _EPS_CACHE
    if _EPS_CACHE is not None:
        return _EPS_CACHE
    eps = {}
    for name in ("A", "Bp", "C", "dt"):
        kw, kb = _TF_KEYS[name]
        eps[name] = (_tf_normal(kw, (D, D)), _tf_normal(kb, (D,)))
    _EPS_CACHE = eps
    return eps


def _common_inputs(ii):
    eps = _eps_constants()
    TT_ = lambda a: np.ascontiguousarray(a.T)
    common = {}
    for P, src in (("A", "A"), ("dt", "dt"), ("B", "Bp"), ("C", "C")):
        common[f"wmuT_{P}"] = TT_(ii[f"L0_attn_a_{src}_wmu"])
        common[f"wlvT_{P}"] = TT_(ii[f"L0_attn_a_{src}_wlv"])
        common[f"epsT_{P}"] = np.ascontiguousarray(eps[src][0].T)
    order = ("A", "dt", "Bp", "C")
    common["bmu_cat"] = np.stack([ii[f"L0_attn_a_{s1}_bmu"] for s1 in order])
    common["blv_cat"] = np.stack([ii[f"L0_attn_a_{s1}_blv"] for s1 in order])
    common["epsb_cat"] = np.stack([eps[s1][1] for s1 in order])
    common["ipwT"] = TT_(ii["L0_attn_b_in_proj_weight"])
    common["ipb"] = ii["L0_attn_b_in_proj_bias"]
    common["opwT"] = TT_(ii["L0_attn_b_out_proj_weight"])
    common["opb"] = ii["L0_attn_b_out_proj_bias"]
    common["Wr"] = ii["L0_ffn_Wr"]
    common["br"] = ii["L0_ffn_br"][None, :]
    w12 = np.concatenate([ii["L0_ffn_W1"], ii["L0_ffn_W2"]], axis=1)
    common["W12"] = np.ascontiguousarray(
        w12.reshape(8, 16, 128, 1024).transpose(0, 2, 1, 3).reshape(8 * 128, 16384))
    common["rowb"] = np.arange(128, dtype=np.float32)[:, None]
    common["b1"] = ii["L0_ffn_b1"]
    common["b2"] = ii["L0_ffn_b2"]
    common["identf"] = np.eye(128, dtype=np.float32)
    common["identb"] = np.eye(128).astype(ml_dtypes.bfloat16)
    common["iota8"] = np.arange(8, dtype=np.float32)[None, :]
    A8 = np.zeros((8, 128), np.float32)
    B8 = np.full((8, 128), -240.0, np.float32)
    for r in range(8):
        A8[r, r * 16:(r + 1) * 16] = 1.0
        B8[r, r * 16:(r + 1) * 16] = 0.0
    common["maskA"] = A8.astype(ml_dtypes.bfloat16)
    common["maskB"] = np.tile(B8, (1, 4)).astype(ml_dtypes.bfloat16)
    return common


def kernel(**inputs):
    ii = {k_: np.asarray(v_) for k_, v_ in inputs.items()}
    common = _common_inputs(ii)
    x = ii["L0_attn_a_x"]
    in_maps = []
    for c in range(NC):
        m = dict(common)
        m["x"] = np.ascontiguousarray(x[2 * c:2 * c + 2])
        m["coreid"] = np.array([[c]], np.int32)
        in_maps.append(m)
    nc = _get_nc()
    res = run_bass_kernel_spmd(nc, in_maps, core_ids=list(range(NC)))
    outp = np.empty((NB, T, D), np.float32)
    for c in range(NC):
        outp[2 * c:2 * c + 2] = res.results[c]["out"]
    return outp


if __name__ == "__main__":
    nc = build()
    print("built ok")
